# version 20 (final)
# baseline (speedup 1.0000x reference)
import numpy as np
import ml_dtypes
import concourse.bass as bass
import concourse.mybir as mybir
from concourse.bass_utils import run_bass_kernel_spmd

try:
    from concourse import compiler_utils as _cu
    _flags = _cu.get_compiler_flags()
    _nf = [f.replace("--enable-ldw-opt=false", "--enable-ldw-opt=true")
           for f in _flags]
    if _nf != _flags:
        _cu.set_compiler_flags(_nf)
except Exception:
    pass

F32 = mybir.dt.float32
BF16 = mybir.dt.bfloat16
I32 = mybir.dt.int32
AF = mybir.ActivationFunctionType
ALU = mybir.AluOpType
AX = mybir.AxisListType
BF = ml_dtypes.bfloat16

B, T, D = 4, 2048, 2048
NH, NKV, HD = 16, 4, 128
G, I, E, GS = 8, 1408, 64, 8
NI = 11
OWN, TOWN = 8, 1024
C_SCALE = float(1.0 / np.sqrt(128.0))
EPS = 1e-5

def _attn_sched():
    sched = {0: [], 1: []}
    slot = 0
    for gq in (0, 1):
        base = 4 * gq
        for kp in range(0, base + 4):
            s = max(0, kp - base)
            sched[gq].append(("own", kp, s, slot)); slot += 1
        for kp in range(8, 8 + base + 4):
            s = max(0, (kp - 8) - base)
            sched[gq].append(("oth", kp, s, slot)); slot += 1
    return sched

SCHED = _attn_sched()
NSLOT = sum(len(v) for v in SCHED.values())
_SLOPES = 2.0 ** (-8.0 * np.arange(1, NH + 1) / NH)

def _active_tiles(h, gq):
    base = 4 * gq
    out = []
    for kind, kp, s, slot in SCHED[gq]:
        jlead = base + s
        if kind == "own":
            dmin = 256 * (jlead - kp) - 127
        else:
            dmin = 256 * (jlead - (kp - 8)) - 255
        if _SLOPES[h] * max(dmin, 0) < 150.0:
            out.append((kind, kp, s, slot))
    return out



def _f32(a):
    return np.ascontiguousarray(a, dtype=np.float32)


def stage_host(inputs):
    x = np.asarray(inputs["L0_attn_x"], dtype=np.float32)
    gate_w = _f32(inputs["L0_ffn_gate_w"])
    mask_up = _f32(inputs["L0_ffn_mask_up_w"])
    inv_map = np.asarray(inputs["L0_ffn_inv_map"]).astype(np.int32)

    muta = np.ascontiguousarray(mask_up.transpose(0, 2, 1).reshape(G * GS, D))
    gmask = np.zeros((64, 8), np.float32)
    for g in range(8):
        gmask[g * 8:(g + 1) * 8, g] = 1.0
    ggrp = np.zeros((64, 8 * 128), np.float32)
    for jp in range(64):
        ggrp[jp, (jp // 8) * 128:((jp // 8) + 1) * 128] = 1.0

    shared = {
        "Wq": _f32(inputs["L0_attn_Wq"]), "Wk": _f32(inputs["L0_attn_Wk"]),
        "Wv": _f32(inputs["L0_attn_Wv"]), "Wo": _f32(inputs["L0_attn_Wo"]),
        "gate_wT": _f32(gate_w.T),
        "mutaT": muta,
        "gateT": _f32(inputs["L0_ffn_gate_proj_w"].transpose(0, 2, 1)),
        "upT": _f32(inputs["L0_ffn_up_proj_w"].transpose(0, 2, 1)),
        "downT": _f32(inputs["L0_ffn_down_proj_w"].transpose(0, 2, 1)),
        "inv_flat": inv_map.reshape(1, E),
        "IDN_BF": np.eye(128, dtype=np.float32).astype(BF),
        "IDN_F32": np.eye(128, dtype=np.float32),
        "ONES_BF": np.ones((128, 128), np.float32).astype(BF),
        "ONES_F32": np.ones((128, 128), np.float32),
        "ones_row": np.ones((1, 128), np.float32),
        "iota64": np.arange(64, dtype=np.float32).reshape(64, 1),
        "Sgrp": np.kron(np.eye(8, dtype=np.float32), np.ones((8, 8), np.float32)).astype(BF),
        "Ggrp": ggrp.astype(BF), "Gmask": gmask.astype(BF), "Sg8": gmask.astype(np.float32),
        "JROT": np.block([[np.zeros((64, 64), np.float32), np.eye(64, dtype=np.float32)],
                          [-np.eye(64, dtype=np.float32), np.zeros((64, 64), np.float32)]]).astype(BF),
    }

    slopes = 2.0 ** (-8.0 * np.arange(1, NH + 1, dtype=np.float32) / NH)
    theta = 1.0 / 10000.0 ** (np.arange(64, dtype=np.float32) / 64.0)

    per_core = []
    for c in range(8):
        b, half = c // 2, c % 2
        own_blocks = [half + 2 * j for j in range(OWN)]
        oth_blocks = [(1 - half) + 2 * j for j in range(OWN)]
        perm_rows = np.concatenate(
            [np.arange(pb * 128, (pb + 1) * 128) for pb in own_blocks + oth_blocks])
        pos = perm_rows.astype(np.float32)
        ang = np.outer(pos, theta)
        cosk = np.cos(ang).T; sink = np.sin(ang).T
        alibi_k = np.zeros((128, 256), np.float32)
        for h in range(NH):
            for kp in range(16):
                alibi_k[:, h * 16 + kp] = slopes[h] * pos[kp * 128:(kp + 1) * 128]
        alibiq = (-slopes[:, None] * pos[None, :TOWN] * np.sqrt(128.0)).astype(BF)
        clampk = np.zeros((128, 256), np.float32)
        for h in range(NH):
            for kp in range(16):
                clampk[:, h * 16 + kp] = (85.0 - slopes[h] * pos[kp * 128:(kp + 1) * 128]) * np.sqrt(128.0)

        tri = (np.arange(128)[:, None] <= np.arange(128)[None, :]).astype(np.float32)
        masks = np.ones((128, NSLOT * 128), np.float32)
        for gq in (0, 1):
            for kind, kp, s, slot in SCHED[gq]:
                jlead = 4 * gq + s
                if kind == "own" and kp == jlead:
                    masks[:, slot * 128:(slot + 1) * 128] = tri
                elif kind == "oth" and (kp - 8) == jlead and half == 0:
                    masks[:, slot * 128:(slot + 1) * 128] = 0.0

        per_core.append({
            "x_perm": _f32(x[b][perm_rows]),
            "COS2_K": _f32(np.vstack([cosk, cosk])),
            "SIN2_K": _f32(np.vstack([sink, sink])),
            "COS2_Q": _f32(np.vstack([cosk[:, :TOWN], cosk[:, :TOWN]])),
            "SIN2_Q": _f32(np.vstack([sink[:, :TOWN], sink[:, :TOWN]])),
            "alibi_k": alibi_k, "alibiq": alibiq, "clampk": clampk,
            "masks": masks.astype(BF),
            "own_blocks": own_blocks,
        })
    return shared, per_core


class Prog:
    def __init__(self, nc):
        self.nc = nc
        self.ops = {k: [] for k in ("te", "ve", "se", "gp", "sy")}
        self.sem = {}
        self.cnt = {}
        for name in ("LD", "WL", "LX0", "LX1", "LX2", "LX3",
                     "WL0", "WL1", "WK0", "WK1",
                     "TE", "VE", "SE", "OUT", "OU0", "OU1", "OU2", "OU3"):
            self.sem[name] = nc.alloc_semaphore(name)
            self.cnt[name] = 0

    def emit(self, eng, fn):
        self.ops[eng].append(fn)

    def inc(self, eng, fn, sem, n=1):
        self.cnt[sem] += n
        h = self.sem[sem]
        self.ops[eng].append(lambda e, fn=fn, h=h, n=n: fn(e).then_inc(h, n))
        return self.cnt[sem]

    def wait(self, eng, sem, val):
        if val <= 0:
            return
        h = self.sem[sem]
        self.ops[eng].append(lambda e, h=h, v=val: e.wait_ge(h, v))

    def dma(self, q, out, in_, sem=None):
        if sem is None:
            sem = "LD" if q == "sy" else "WL"
        if sem not in ("LD", "OUT") and self.cnt.get(sem, 0) > 0:
            self.wait(q, sem, self.cnt[sem])
        return self.inc(q, lambda e, o=out, i=in_: e.dma_start(out=o, in_=i), sem, 16)

    def outdma(self, out, in_, sem="OUT"):
        if sem != "OUT" and self.cnt.get(sem, 0) > 0:
            self.wait("sy", sem, self.cnt[sem])
        return self.inc("sy", lambda e, o=out, i=in_: e.dma_start(out=o, in_=i), sem, 16)


def build(taps=(), upto=4, sub=99):
    nc = bass.Bass("TRN2", target_bir_lowering=False)
    dr = {}
    def din(name, shape, dtype):
        dr[name] = nc.dram_tensor(name, list(shape), dtype, kind="ExternalInput").ap()

    din("x_perm", (T, D), F32)
    din("Wq", (D, D), F32); din("Wk", (D, 512), F32); din("Wv", (D, 512), F32)
    din("Wo", (D, D), F32)
    din("gate_wT", (D, E), F32); din("mutaT", (64, D), F32)
    din("gateT", (G, D, I), F32); din("upT", (G, D, I), F32); din("downT", (G, I, D), F32)
    din("inv_flat", (1, E), I32)
    din("IDN_BF", (128, 128), BF16); din("IDN_F32", (128, 128), F32)
    din("ONES_BF", (128, 128), BF16); din("ONES_F32", (128, 128), F32)
    din("ones_row", (1, 128), F32); din("iota64", (64, 1), F32)
    din("Sgrp", (64, 64), BF16); din("Ggrp", (64, 8 * 128), BF16); din("Gmask", (64, 8), BF16)
    din("Sg8", (64, 8), F32)
    din("COS2_K", (128, T), F32); din("SIN2_K", (128, T), F32)
    din("COS2_Q", (128, TOWN), F32); din("SIN2_Q", (128, TOWN), F32)
    din("alibi_k", (128, 256), F32); din("alibiq", (NH, TOWN), BF16)
    din("clampk", (128, 256), F32)
    din("JROT", (128, 128), BF16)
    din("masks", (128, NSLOT * 128), BF16)
    out = nc.dram_tensor("out", [TOWN, D], F32, kind="ExternalOutput").ap()

    AR = nc.alloc_sbuf_tensor("AR", [128, 212352], mybir.dt.uint8)
    ARB = AR.ap().bitcast(BF16)
    ARF = AR.ap().bitcast(F32)
    def bfv(off_b, n):
        assert off_b % 2 == 0
        return ARB[:, off_b // 2: off_b // 2 + n]
    def f32v(off_b, n):
        assert off_b % 4 == 0
        return ARF[:, off_b // 4: off_b // 4 + n]

    XLN  = bfv(0, 16 * 2048)
    MASKS = bfv(0, NSLOT * 128)
    ALQ  = [bfv(8192, 1024), bfv(12288, 1024)]
    X1T  = f32v(0, 16 * 1024)
    QT   = bfv(65536, 16 * 1024)
    XG   = bfv(65536, 16 * 1024)
    COS2Q = f32v(98304, 1024); SIN2Q = f32v(102400, 1024)
    COS2K = f32v(106496, 2048); SIN2K = f32v(114688, 2048)
    OT   = bfv(98304, 16 * 1024)
    XLN2 = bfv(98304, 16 * 1024)
    KT   = bfv(131072, 4 * 2048)
    VTOK = bfv(147456, 4 * 2048)
    APR  = bfv(131072, NI * 1024)
    WET  = bfv(153600, 1024)
    FLT  = bfv(157696, 1024)
    SOFTT = bfv(161792, 1024)
    XL   = [f32v(163840, 1024), f32v(167936, 1024),
            f32v(65536, 1024), f32v(69632, 1024)]
    NUMT = bfv(163840, 1024)
    GWT  = f32v(167936, 1024)
    SCBC = f32v(167936, 1024)
    RING = 172032
    WKV  = [bfv(172032, 2048), bfv(176128, 2048)]
    VTTF = f32v(180224, 2048)
    RAW = bfv(188416, 1024)
    VJUNK = f32v(180224, 1024)
    HJUNK = f32v(188416, 1024)
    NEGMR = f32v(188416, 1024)
    ATMP = f32v(188416, 1024)
    WSL  = [bfv(192512, 2048), bfv(196608, 2048)]
    XLNORMP = [f32v(192512, 1024), f32v(196608, 1024)]
    GGRP = bfv(192512, 1024)
    MUTA = bfv(196608, 2048)
    co = 200704
    def cst(nbytes):
        nonlocal co
        o = co; co += (nbytes + 63) & ~63
        return o
    IDNBF = bfv(cst(256), 128)
    JROT = bfv(cst(256), 128)
    IDNF  = f32v(cst(512), 128)
    ONESBF = bfv(cst(256), 128)
    ONESF  = f32v(cst(512), 128)
    ONESROW = f32v(cst(512), 128)
    ALIBIK = f32v(cst(1024), 256)
    CLAMPK = f32v(cst(1024), 256)
    IOTA64 = f32v(cst(64), 1)
    SGRP = bfv(cst(128), 64)
    PERM = bfv(cst(128), 64)
    INVF = f32v(cst(256), 64)
    GMASK = bfv(cst(64), 8)
    SOFTG = bfv(cst(2048), 1024)
    RCOLS = f32v(cst(64), 8)
    SG8 = f32v(cst(64), 8)
    SCR  = f32v(cst(256), 64)
    assert co <= 212352, co

    PS = [nc.alloc_psum_tensor(f"PS{k}", [128, 1024], F32).ap() for k in range(4)]

    p = Prog(nc)
    taps_avail = {}

    for s_, d_ in [(IDNBF, dr["IDN_BF"]), (IDNF, dr["IDN_F32"]), (JROT, dr["JROT"]),
                   (ONESBF, dr["ONES_BF"]), (ONESF, dr["ONES_F32"]),
                   (ONESROW[0:1, :], dr["ones_row"]),
                   (ALIBIK, dr["alibi_k"]), (CLAMPK, dr["clampk"]),
                   (IOTA64[0:64, :], dr["iota64"]),
                   (SGRP[0:64, :], dr["Sgrp"]), (GMASK[0:64, :], dr["Gmask"]),
                   (SG8[0:64, :], dr["Sg8"]),
                   (COS2K, dr["COS2_K"]), (SIN2K, dr["SIN2_K"]),
                   (COS2Q, dr["COS2_Q"]), (SIN2Q, dr["SIN2_Q"])]:
        ld_const = p.dma("sy", s_, d_)
    wl_inv = p.dma("gp", INVF[0:1, 0:64], dr["inv_flat"])

    x_r = dr["x_perm"].rearrange("(tt p) (hc q) -> tt hc p q", p=128, q=1024)
    xld = []
    ve_xl_free = [0, 0, 0, 0]
    se_xl_free = [0, 0, 0, 0]
    te_norm_free = [0, 0]
    cS0, cS1, cQ0, cQ1, cSUM, cSQ, cM, cMSQ, cVAR, cR, cNMR, cTMP = range(12)
    cEPS = 62

    def scr(col):
        return SCR[:, col:col + 1]

    ldc = p.cnt["LD"]
    p.wait("ve", "LD", ldc)
    p.wait("se", "LD", ldc)
    p.wait("te", "LD", ldc)
    p.emit("ve", lambda v: v.memset(SCR[:, 62:63], EPS))
    prev_ve_tok = [0, 0, 0, 0]
    prev_sq_tok = [0, 0, 0, 0]
    prev_rtok = [0, 0, 0, 0]
    prev_nv_tok = [0, 0]
    tpose_done = []
    evac_done = []
    last_norm_te = 0
    evac_prev = [0, 0]
    exp_tok = [0] * 16
    rtok_t = [[0, 0] for _ in range(16)]

    def scr(col, par=None, _S=SCR):
        return _S[:, col:col + 1]

    def scrp(col, par):
        return SCR[:, 12 * par + col:12 * par + col + 1]

    def emit_load(tt):
        for hc in (0, 1):
            slot = (tt * 2 + hc) % 4
            p.wait("sy", "VE", ve_xl_free[slot])
            p.wait("sy", "SE", se_xl_free[slot])
            xld.append((f"LX{slot}", p.dma("sy", XL[slot], x_r[tt, hc], sem=f"LX{slot}")))

    def emit_stats_sums(tt):
        par = tt % 4
        for hc in (0, 1):
            slot = (tt * 2 + hc) % 4
            p.wait("se", *xld[tt * 2 + hc])
            p.wait("se", "VE", prev_ve_tok[par])
            p.wait("se", "SE", prev_sq_tok[par])
            rtok = p.inc("se", lambda s_, sl=slot, par=par, c=(cS0, cS1)[hc]: s_.activation(
                HJUNK, XL[sl], AF.Copy, accum_out=scrp(c, par)), "SE")
            prev_sq_tok[par] = rtok
            rtok_t[tt][hc] = rtok
            se_xl_free[slot] = rtok

    def emit_stats(tt):
        par = tt % 4
        for hc in (0, 1):
            slot = (tt * 2 + hc) % 4
            p.wait("ve", *xld[tt * 2 + hc])
            p.wait("ve", "VE", prev_rtok[par])
            sq_tok = p.inc("ve", lambda v, sl=slot, par=par, c=(cQ0, cQ1)[hc]:
                v.scalar_tensor_tensor(
                    out=VJUNK, in0=XL[sl], scalar=1.0, in1=XL[sl],
                    op0=ALU.mult, op1=ALU.mult, accum_out=scrp(c, par)), "VE")
            prev_rtok[par] = sq_tok
        rtok = rtok_t[tt][1]
        p.wait("ve", "SE", rtok)
        t1 = p.inc("ve", lambda v, par=par: v.tensor_tensor(
            out=scrp(cSUM, par), in0=scrp(cS0, par), in1=scrp(cS1, par), op=ALU.add), "VE")
        p.wait("ve", "VE", sq_tok)
        t2 = p.inc("ve", lambda v, par=par: v.tensor_tensor(
            out=scrp(cSQ, par), in0=scrp(cQ0, par), in1=scrp(cQ1, par), op=ALU.add), "VE")
        p.wait("ve", "VE", t1)
        t3 = p.inc("ve", lambda v, par=par: v.tensor_scalar(out=scrp(cM, par), in0=scrp(cSUM, par),
                   scalar1=1.0 / D, scalar2=None, op0=ALU.mult), "VE")
        p.wait("ve", "VE", t3)
        t4 = p.inc("ve", lambda v, par=par: v.tensor_scalar(out=scrp(cMSQ, par), in0=scrp(cM, par),
                   scalar1=scrp(cM, par), scalar2=None, op0=ALU.mult), "VE")
        p.wait("ve", "VE", t4)
        ve_tok = p.inc("ve", lambda v, par=par: v.scalar_tensor_tensor(
            out=scrp(cVAR, par), in0=scrp(cSQ, par), scalar=1.0 / D, in1=scrp(cMSQ, par),
            op0=ALU.mult, op1=ALU.subtract), "VE")
        prev_ve_tok[par] = ve_tok
        p.wait("se", "VE", ve_tok)
        s1 = p.inc("se", lambda s_, par=par: s_.activation(scrp(cTMP, par), scrp(cVAR, par), AF.Ln,
                   bias=scr(cEPS)), "SE")
        p.wait("se", "SE", s1)
        se_tok = p.inc("se", lambda s_, par=par: s_.activation(scrp(cR, par), scrp(cTMP, par), AF.Exp,
                       scale=-0.5), "SE")
        exp_tok[tt] = se_tok

    def emit_finish(tt):
        par = tt % 4
        p.wait("ve", "SE", exp_tok[tt])
        nmr_tok = p.inc("ve", lambda v, par=par: v.tensor_scalar(
            out=scrp(cNMR, par), in0=scrp(cM, par), scalar1=scrp(cR, par), scalar2=-1.0,
            op0=ALU.mult, op1=ALU.mult), "VE")
        p.wait("ve", "VE", nmr_tok)
        for hc in (0, 1):
            slot = (tt * 2 + hc) % 4
            p.wait("ve", "TE", te_norm_free[hc])
            nv_tok = p.inc("ve", lambda v, sl=slot, par=par, hc=hc: v.tensor_scalar(
                out=XLNORMP[hc], in0=XL[sl], scalar1=scrp(cR, par), scalar2=scrp(cNMR, par),
                op0=ALU.mult, op1=ALU.add), "VE")
            prev_nv_tok[hc] = nv_tok
            ve_xl_free[slot] = nv_tok
        for hc in (0, 1):
            nv_tok = prev_nv_tok[hc]
            p.wait("te", "VE", nv_tok)
            ps = PS[2 + hc]
            p.wait("te", "SE", evac_prev[hc])
            for j in range(8):
                fn = (lambda t_, j=j, ps=ps, hc=hc: nc.tensor.transpose(
                    ps[:, j * 128:(j + 1) * 128],
                    XLNORMP[hc][:, j * 128:(j + 1) * 128], IDNF))
                if j == 7:
                    te_norm_free[hc] = p.inc("te", fn, "TE")
                else:
                    p.emit("te", fn)
            p.wait("se", "TE", te_norm_free[hc])
            ev = p.inc("se", lambda s_, ps=ps, hc=hc, tt=tt: s_.copy(
                out=XLN.rearrange("p (hh q) -> p hh q", q=2048)[:, hc * 8:hc * 8 + 8,
                    tt * 128:(tt + 1) * 128],
                in_=ps.rearrange("p (a b) -> p a b", b=128)), "SE")
            evac_prev[hc] = ev
            evac_done.append(ev)

    emit_load(0)
    emit_load(1)
    for step in range(17):
        if step < 16:
            emit_stats_sums(step)
        if step >= 1:
            emit_finish(step - 1)
        if step >= 1 and step + 1 < 16:
            emit_load(step + 1)
        if step < 16:
            emit_stats(step)
    p1a_te = p.cnt["TE"]; p1a_ve = p.cnt["VE"]
    taps_avail["XLN"] = (XLN, BF16, "SE", p.cnt["SE"])
    q_rope_done = kv_rope_done = 0
    if sub < 1:
        upto = 0
    if sub >= 1:

        wk_r = dr["Wk"].rearrange("(hh p) (c q) -> c p hh q", p=128, q=128)
        wv_r = dr["Wv"].rearrange("(hh p) (c q) -> c p hh q", p=128, q=128)
        kv_wl = []
        te_wkv_free = [0, 0]
        ps01_free_ve = 0
        kv_rope_done = 0
        raw_free_te = 0
        raw_free_ve = 0
        raw_prev_se = 0
        vttf_prev_ve = p1a_ve
        for c in range(4):
            slot = c % 2
            p.wait("gp", "TE", te_wkv_free[slot])
            wl = p.dma("gp", WKV[slot].rearrange("p (hh q) -> p hh q", q=128), wk_r[c],
                       sem=f"WK{slot}")
            p.wait("te", f"WK{slot}", wl)
            p.wait("te", "VE", ps01_free_ve)
            tok = 0
            for tc in range(4):
                if c == 0:
                    p.wait("te", "SE", evac_done[(tc * 4 + 3) * 2 + 1])
                ps = PS[tc // 2]
                for hh in range(16):
                    fn = (lambda t_, hh=hh, tc=tc, ps=ps, sl=slot: nc.tensor.matmul(
                        ps[:, (tc % 2) * 512:(tc % 2) * 512 + 512],
                        WKV[sl][:, hh * 128:(hh + 1) * 128],
                        XLN[:, hh * 2048 + tc * 512: hh * 2048 + tc * 512 + 512],
                        start=(hh == 0), stop=(hh == 15)))
                    if hh == 15 and tc == 3:
                        tok = p.inc("te", fn, "TE")
                    else:
                        p.emit("te", fn)
            te_wkv_free[slot] = tok
            for ch in range(2):
                psK = PS[ch]
                psW = PS[2 + ch]
                sl = slice(ch * 1024, ch * 1024 + 1024)
                p.wait("se", "TE", tok)
                p.wait("se", "TE", raw_free_te)
                p.wait("se", "VE", raw_free_ve)
                p.wait("se", "SE", raw_prev_se)
                se_tok = p.inc("se", lambda s_, psK=psK: s_.copy(out=RAW, in_=psK[:]), "SE")
                raw_prev_se = se_tok
                p.wait("te", "SE", se_tok)
                jtok = 0
                for half_ in range(2):
                    fn = (lambda t_, psW=psW, half_=half_: nc.tensor.matmul(
                        psW[:, half_ * 512:half_ * 512 + 512], JROT,
                        RAW[:, half_ * 512:half_ * 512 + 512], start=True, stop=True))
                    jtok = p.inc("te", fn, "TE") if half_ == 1 else (p.emit("te", fn) or jtok)
                raw_free_te = jtok
                p.wait("ve", "SE", se_tok)
                p.wait("ve", "VE", vttf_prev_ve)
                raw_free_ve = p.inc("ve", lambda v, sl=sl: v.tensor_tensor(
                    out=VTTF[:, 0:1024], in0=RAW, in1=COS2K[:, sl], op=ALU.mult), "VE")
                p.wait("ve", "TE", jtok)
                sm_tok = p.inc("ve", lambda v, psW=psW, sl=sl: v.tensor_tensor(
                    out=VTTF[:, 1024:2048], in0=psW[:], in1=SIN2K[:, sl], op=ALU.mult), "VE")
                p.wait("ve", "VE", sm_tok)
                ev = p.inc("ve", lambda v, c=c, sl=sl: v.tensor_tensor(
                    out=KT[:, c * 2048 + sl.start: c * 2048 + sl.stop],
                    in0=VTTF[:, 0:1024], in1=VTTF[:, 1024:2048], op=ALU.add), "VE")
                vttf_prev_ve = ev
            ps01_free_ve = ev
            kv_rope_done = ev
        taps_avail["KT"] = (KT, BF16, "VE", p.cnt["VE"])

        for c in range(4 if sub >= 2 else 0):
            slot = c % 2
            p.wait("gp", "TE", te_wkv_free[slot])
            wl = p.dma("gp", WKV[slot].rearrange("p (hh q) -> p hh q", q=128), wv_r[c],
                       sem=f"WK{slot}")
            p.wait("te", f"WK{slot}", wl)
            p.wait("te", "VE", ps01_free_ve)
            tok = 0
            for tc in range(4):
                ps = PS[tc // 2]
                for hh in range(16):
                    fn = (lambda t_, hh=hh, tc=tc, ps=ps, sl=slot: nc.tensor.matmul(
                        ps[:, (tc % 2) * 512:(tc % 2) * 512 + 512],
                        WKV[sl][:, hh * 128:(hh + 1) * 128],
                        XLN[:, hh * 2048 + tc * 512: hh * 2048 + tc * 512 + 512],
                        start=(hh == 0), stop=(hh == 15)))
                    if hh == 15 and tc == 3:
                        tok = p.inc("te", fn, "TE")
                    else:
                        p.emit("te", fn)
            te_wkv_free[slot] = tok
            p.wait("ve", "TE", tok)
            p.wait("ve", "VE", vttf_prev_ve)
            ev0 = p.inc("ve", lambda v: v.tensor_copy(out=VTTF[:, 0:1024], in_=PS[0][:]), "VE")
            ev1 = p.inc("ve", lambda v: v.tensor_copy(out=VTTF[:, 1024:2048], in_=PS[1][:]), "VE")
            ps01_free_ve = ev1
            p.wait("te", "VE", ev1)
            for grp in (0, 1):
                ps = PS[2 + grp]
                for j in range(8):
                    tb = grp * 8 + j
                    fn = (lambda t_, tb=tb, j=j, ps=ps: nc.tensor.transpose(
                        ps[:, j * 128:(j + 1) * 128],
                        VTTF[:, tb * 128:(tb + 1) * 128], IDNF))
                    if j == 7:
                        ttok = p.inc("te", fn, "TE")
                    else:
                        p.emit("te", fn)
                p.wait("ve", "TE", ttok)
                ev = p.inc("ve", lambda v, ps=ps, c=c, grp=grp: v.tensor_copy(
                    out=VTOK.rearrange("p (tt f) -> p tt f", f=512)[:, grp * 8:grp * 8 + 8,
                        c * 128:(c + 1) * 128],
                    in_=ps.rearrange("p (a b) -> p a b", b=128)), "VE")
                p.wait("te", "VE", ev)
        taps_avail["VTOK"] = (VTOK, BF16, "VE", p.cnt["VE"])

        wq_r = dr["Wq"].rearrange("(hh p) (h q) -> h p hh q", p=128, q=128)
        p.wait("ve", "LX2", p.cnt["LX2"])
        p.wait("ve", "LX3", p.cnt["LX3"])
        te_wq_free = [p1a_te, p1a_te]
        q_evac = [0] * NH
        for h in range(NH if sub >= 3 else 0):
            slot = h % 2
            p.wait("gp", "TE", te_wq_free[slot])
            wl = p.dma("gp", WSL[slot].rearrange("p (hh q) -> p hh q", q=128), wq_r[h],
                       sem=f"WL{slot}")
            p.wait("te", f"WL{slot}", wl)
            ps = PS[h % 2]
            p.wait("te", "VE", ps01_free_ve if h < 2 else q_evac[h - 2])
            tok = 0
            for hh in range(16):
                for tc in range(2):
                    fn = (lambda t_, hh=hh, tc=tc, ps=ps, sl=slot: nc.tensor.matmul(
                        ps[:, tc * 512: tc * 512 + 512],
                        WSL[sl][:, hh * 128:(hh + 1) * 128],
                        XLN[:, hh * 2048 + tc * 512: hh * 2048 + tc * 512 + 512],
                        start=(hh == 0), stop=(hh == 15)))
                    if hh == 15 and tc == 1:
                        tok = p.inc("te", fn, "TE")
                    else:
                        p.emit("te", fn)
            te_wq_free[slot] = tok
            psW = PS[2 + h % 2]
            p.wait("se", "TE", tok)
            p.wait("se", "TE", raw_free_te)
            p.wait("se", "VE", raw_free_ve)
            p.wait("se", "SE", raw_prev_se)
            se_tok = p.inc("se", lambda s_, ps=ps: s_.copy(out=RAW, in_=ps[:]), "SE")
            raw_prev_se = se_tok
            p.wait("te", "SE", se_tok)
            jtok = 0
            for half_ in range(2):
                fn = (lambda t_, psW=psW, half_=half_: nc.tensor.matmul(
                    psW[:, half_ * 512:half_ * 512 + 512], JROT,
                    RAW[:, half_ * 512:half_ * 512 + 512], start=True, stop=True))
                jtok = p.inc("te", fn, "TE") if half_ == 1 else (p.emit("te", fn) or jtok)
            raw_free_te = jtok
            p.wait("ve", "SE", se_tok)
            p.wait("ve", "VE", vttf_prev_ve)
            raw_free_ve = p.inc("ve", lambda v: v.tensor_tensor(
                out=VTTF[:, 0:1024], in0=RAW, in1=COS2Q, op=ALU.mult), "VE")
            p.wait("ve", "TE", jtok)
            sm_tok = p.inc("ve", lambda v, psW=psW: v.tensor_tensor(
                out=VTTF[:, 1024:2048], in0=psW[:], in1=SIN2Q, op=ALU.mult), "VE")
            p.wait("ve", "VE", sm_tok)
            ev = p.inc("ve", lambda v, h=h: v.tensor_tensor(
                out=QT[:, h * 1024:(h + 1) * 1024],
                in0=VTTF[:, 0:1024], in1=VTTF[:, 1024:2048], op=ALU.add), "VE")
            vttf_prev_ve = ev
            q_evac[h] = ev
        taps_avail["QT"] = (QT, BF16, "VE", p.cnt["VE"])
        q_rope_done = q_evac[NH - 1] if sub >= 3 else 0
    p1_te = p.cnt["TE"]; p1_ve = p.cnt["VE"]

    state = dict(nc=nc, p=p, dr=dr, out=out, PS=PS, taps_avail=taps_avail,
                 XLN=XLN, MASKS=MASKS, ALQ=ALQ, X1T=X1T, QT=QT, XG=XG,
                 OT=OT, XLN2=XLN2, KT=KT, VTOK=VTOK, APR=APR, WET=WET, FLT=FLT,
                 SOFTT=SOFTT, XL=XL, NUMT=NUMT, GWT=GWT, SCBC=SCBC, RING=RING,
                 RAW=RAW, NEGMR=NEGMR, ATMP=ATMP, WSL=WSL, GGRP=GGRP, MUTA=MUTA,
                 IDNBF=IDNBF, IDNF=IDNF, ONESBF=ONESBF, ONESF=ONESF,
                 ONESROW=ONESROW, ALIBIK=ALIBIK, CLAMPK=CLAMPK, IOTA64=IOTA64, SGRP=SGRP,
                 PERM=PERM, INVF=INVF, GMASK=GMASK, SOFTG=SOFTG, RCOLS=RCOLS,
                 WT2=NUMT, SG8=SG8,
                 SCR=SCR, bfv=bfv, f32v=f32v, wl_inv=wl_inv,
                 p1_tokens=dict(q_rope_done=q_rope_done, kv_rope_done=kv_rope_done,
                                p1_te=p1_te, p1_ve=p1_ve))

    if upto >= 2:
        build_p2(state)
    if upto >= 3:
        build_p3(state)
    if upto >= 4:
        build_p4(state)

    tap_out = {}
    for name in taps:
        buf, dt_, sem, val = state["taps_avail"][name]
        tshape = [128, buf.shape[1]]
        tap_out[name] = nc.dram_tensor("tap_" + name, tshape, dt_, kind="ExternalOutput").ap()
        p.wait("sy", sem, val)
        p.outdma(tap_out[name], buf)

    if upto < 4:
        p.wait("ve", "OUT", p.cnt["OUT"])
        tok = p.inc("ve", lambda v: v.memset(f32v(0, 2048), 0.0), "VE")
        p.wait("sy", "VE", tok)
        o_r = out.rearrange("(tt p) d -> tt p d", p=128)
        for tt in range(8):
            p.outdma(o_r[tt], f32v(0, 2048))
    for s in ("LD", "WL", "LX0", "LX1", "LX2", "LX3",
              "WL0", "WL1", "WK0", "WK1", "OU0", "OU1", "OU2", "OU3"):
        p.wait("sy", s, p.cnt[s])
    p.wait("sy", "OUT", p.cnt["OUT"])

    with nc.Block() as blk:
        @blk.sync
        def _(sy):
            for fn in p.ops["sy"]:
                fn(sy)
        @blk.gpsimd
        def _(gp):
            for fn in p.ops["gp"]:
                fn(gp)
        @blk.tensor
        def _(te):
            for fn in p.ops["te"]:
                fn(te)
        @blk.vector
        def _(ve):
            for fn in p.ops["ve"]:
                fn(ve)
        @blk.scalar
        def _(se):
            for fn in p.ops["se"]:
                fn(se)
    return nc, [k for k in tap_out]


def build_p2(st):
    p = st["p"]; nc = st["nc"]; dr = st["dr"]; PS = st["PS"]
    QT = st["QT"]; KT = st["KT"]; VTOK = st["VTOK"]; OT = st["OT"]
    MASKS = st["MASKS"]; ALQ = st["ALQ"]
    ONESBF = st["ONESBF"]; ALIBIK = st["ALIBIK"]
    bfv = st["bfv"]; f32v = st["f32v"]
    tk = st["p1_tokens"]

    PSB = [bfv(172032 + k * 1024, 512) for k in range(6)]
    RECIPB = f32v(188416, 512)

    p.wait("sy", "TE", tk["p1_te"])
    masks_ld = p.dma("sy", MASKS, dr["masks"])
    lx2_final = p.cnt["LX2"]; lx3_final = p.cnt["LX3"]
    aq_sem = ["AQ0", "AQ1"]
    for s in aq_sem:
        if s not in p.sem:
            p.sem[s] = nc.alloc_semaphore(s); p.cnt[s] = 0
    alq_ld = [0, 0]
    alq_ld[0] = p.dma("sy", ALQ[0][0:1, :], dr["alibiq"][0:1, :], sem="AQ0")
    alq_ld[1] = p.dma("sy", ALQ[1][0:1, :], dr["alibiq"][1:2, :], sem="AQ1")

    SCB = [PS[0][:, 0:512], PS[0][:, 512:1024], PS[2][:, 0:512], PS[2][:, 512:1024]]
    slot_exp = [0, 0, 0, 0]
    psb_av = [0] * 6
    first_wait_done = False
    prev_recip_ve = 0
    slot_psb_exp = [0] * 6
    unit_evac = {}
    unit = 0
    for h in range(NH):
        kv = h // 4
        aq = h % 2
        for gq in (0, 1):
            u = unit; unit += 1
            OS = PS[1] if u % 2 == 0 else PS[3]
            sched = _active_tiles(h, gq)
            ntile = len(sched)
            exp_t = [0] * ntile
            mask_t = [0] * ntile
            sc_mm_t = [0] * ntile
            clamp_t = [0] * ntile
            avsum_t = [0] * ntile
            for step in range(ntile + 3):
                i = step
                if i < ntile:
                    kind, kp, s, slot = sched[i]
                    ncols = 512 - 128 * s
                    qoff = h * 1024 + gq * 512 + 128 * s
                    kcol = kv * 2048 + kp * 128
                    sl4 = (u * 24 + i) % 4
                    sl6 = (u * 24 + i) % 6
                    scs = SCB[sl4]
                    need_r1 = _SLOPES[h] * 2047.0 > 70.0
                    p.wait("te", "SE", slot_exp[sl4])
                    if i == 0:
                        if u >= 2:
                            p.wait("te", "VE", unit_evac[u - 2])
                        if need_r1:
                            p.wait("te", aq_sem[aq], alq_ld[aq])
                        if not first_wait_done:
                            p.wait("te", "VE", tk["q_rope_done"])
                            first_wait_done = True
                    if need_r1:
                        p.emit("te", lambda t_, scs=scs, kcol=kcol, qoff=qoff, ncols=ncols:
                               nc.tensor.matmul(scs[:, 0:ncols], KT[:, kcol:kcol + 128],
                                                QT[:, qoff:qoff + ncols],
                                                start=True, stop=False))
                        sc_mm_t[i] = p.inc("te", lambda t_, scs=scs, aq=aq, gq=gq, s=s,
                               ncols=ncols: nc.tensor.matmul(
                                   scs[:, 0:ncols], ONESBF[0:1, 0:128],
                                   ALQ[aq][0:1, gq * 512 + 128 * s:
                                           gq * 512 + 128 * s + ncols],
                                   start=False, stop=True), "TE")
                    else:
                        sc_mm_t[i] = p.inc("te", lambda t_, scs=scs, kcol=kcol,
                               qoff=qoff, ncols=ncols: nc.tensor.matmul(
                                   scs[:, 0:ncols], KT[:, kcol:kcol + 128],
                                   QT[:, qoff:qoff + ncols],
                                   start=True, stop=True), "TE")
                    if h <= 3:
                        p.wait("ve", "TE", sc_mm_t[i])
                        clamp_t[i] = p.inc("ve", lambda v, scs=scs, h=h, kp=kp:
                            v.tensor_scalar_min(
                                scs[:, 0:128], scs[:, 0:128],
                                st["CLAMPK"][:, h * 16 + kp: h * 16 + kp + 1]), "VE")
                    p.wait("se", "TE", sc_mm_t[i])
                    if clamp_t[i]:
                        p.wait("se", "VE", clamp_t[i])
                    p.wait("se", "TE", psb_av[sl6])
                    p.wait("se", "SE", slot_psb_exp[sl6])
                    if u == 0 and i < 6:
                        p.wait("se", "LD", masks_ld)
                        p.wait("se", "LX2", lx2_final)
                        p.wait("se", "LX3", lx3_final)
                        p.wait("se", "VE", tk["q_rope_done"])
                        p.wait("se", "WK0", p.cnt["WK0"])
                        p.wait("se", "WK1", p.cnt["WK1"])
                    exp_t[i] = p.inc("se", lambda s_, scs=scs, h=h, kp=kp,
                           ncols=ncols, sl6=sl6: s_.activation(
                               PSB[sl6][:, 0:ncols], scs[:, 0:ncols], AF.Exp,
                               bias=ALIBIK[:, h * 16 + kp: h * 16 + kp + 1],
                               scale=C_SCALE), "SE")
                    slot_exp[sl4] = exp_t[i]
                    slot_psb_exp[sl6] = exp_t[i]
                    p.wait("ve", "SE", exp_t[i])
                    if u == 0 and i < 4:
                        p.wait("ve", "LD", masks_ld)
                    mask_t[i] = p.inc("ve", lambda v, sl6=sl6, slot=slot: v.tensor_tensor(
                        out=PSB[sl6][:, 0:128], in0=PSB[sl6][:, 0:128],
                        in1=MASKS[:, slot * 128:(slot + 1) * 128], op=ALU.mult), "VE")
                j = step - 3
                if 0 <= j < ntile:
                    kindj, kpj, sj, slotj = sched[j]
                    ncolsj = 512 - 128 * sj
                    sl4j = (u * 24 + j) % 6
                    p.wait("te", "VE", mask_t[j])
                    first = (j == 0)
                    last = (j == ntile - 1)
                    p.emit("te", lambda t_, OS=OS, sj=sj, kpj=kpj, kv=kv, sl4j=sl4j,
                           ncolsj=ncolsj, first=first, last=last: nc.tensor.matmul(
                               OS[:, 128 * sj:512],
                               VTOK[:, (kpj * 4 + kv) * 128:(kpj * 4 + kv + 1) * 128],
                               PSB[sl4j][:, 0:ncolsj], start=first, stop=last))
                    avsum_t[j] = p.inc("te", lambda t_, OS=OS, sj=sj, sl4j=sl4j,
                           ncolsj=ncolsj, first=first, last=last: nc.tensor.matmul(
                               OS[:, 512 + 128 * sj:1024], ONESBF,
                               PSB[sl4j][:, 0:ncolsj], start=first, stop=last), "TE")
                    psb_av[sl4j] = avsum_t[j]
            p.wait("ve", "TE", avsum_t[ntile - 1])
            p.wait("ve", "VE", prev_recip_ve)
            r1 = p.inc("ve", lambda v, OS=OS: v.reciprocal(
                out=RECIPB, in_=OS[:, 512:1024]), "VE")
            prev_recip_ve = r1
            p.wait("ve", "VE", r1)
            ev = p.inc("ve", lambda v, OS=OS, h=h, gq=gq: v.tensor_tensor(
                out=OT[:, h * 1024 + gq * 512: h * 1024 + gq * 512 + 512],
                in0=OS[:, 0:512], in1=RECIPB, op=ALU.mult), "VE")
            unit_evac[u] = ev
        if h + 1 < NH and _SLOPES[h + 1] * 2047.0 > 70.0:
            nxt = (h + 1) % 2
            if h >= 1:
                p.wait("sy", "VE", unit_evac[(h - 1) * 2 + 1])
            alq_ld[nxt] = p.dma("sy", ALQ[nxt][0:1, :],
                                dr["alibiq"][h + 1:h + 2, :], sem=aq_sem[nxt])
    st["taps_avail"]["OT"] = (OT, BF16, "VE", p.cnt["VE"])
    st["p2_tokens"] = dict(last_evac=unit_evac[unit - 1], p2_te=p.cnt["TE"],
                           p2_ve=p.cnt["VE"])


def build_p3(st):
    p = st["p"]; nc = st["nc"]; dr = st["dr"]; PS = st["PS"]
    OT = st["OT"]; X1T = st["X1T"]; XLN2 = st["XLN2"]; XG = st["XG"]
    XL = st["XL"]; WSL = st["WSL"]; GWT = st["GWT"]
    ONESF = st["ONESF"]; ONESBF = st["ONESBF"]; ONESROW = st["ONESROW"]
    IDNF = st["IDNF"]; IOTA64 = st["IOTA64"]; SGRP = st["SGRP"]
    PERM = st["PERM"]; INVF = st["INVF"]; NEGMR = st["NEGMR"]; RCOLS = st["RCOLS"]
    WET = st["WET"]; FLT = st["FLT"]; SOFTT = st["SOFTT"]; NUMT = st["NUMT"]
    WT2 = st["WT2"]; SG8 = st["SG8"]; SCR = st["SCR"]
    assert WT2 is NUMT or True
    f32v = st["f32v"]
    tk1 = st["p1_tokens"]; tk2 = st["p2_tokens"]
    x_r = dr["x_perm"].rearrange("(tt p) (hc q) -> tt hc p q", p=128, q=1024)
    wo_r = dr["Wo"].rearrange("(dh p) (h2 q) -> h2 p dh q", p=128, q=128)

    MS = f32v(131072, 1024)
    RSTD = f32v(135168, 1024)
    VTMP = f32v(139264, 1024)
    ES = f32v(143360, 512)
    RZ = f32v(145408, 1024)
    attn_ve = tk2["p2_ve"]; attn_te = tk2["p2_te"]

    xld = []
    lx_free = [("VE", 0), ("VE", 0)]
    tp_tok = {}
    copy_tok = {}
    for tt in range(OWN):
        for hc in (0, 1):
            slot = hc
            eng, val = lx_free[slot]
            p.wait("sy", eng, val)
            xld.append((f"LX{slot}", p.dma("sy", XL[slot], x_r[tt, hc], sem=f"LX{slot}")))
            ps = PS[2 + hc]
            p.wait("te", *xld[-1])
            if tt == 0:
                p.wait("te", "VE", tk2["last_evac"])
            if tt >= 1:
                p.wait("te", "SE", copy_tok[(tt - 1, hc)])
            for j in range(8):
                fn = (lambda t_, j=j, ps=ps, sl=slot: nc.tensor.transpose(
                    ps[:, j * 128:(j + 1) * 128], XL[sl][:, j * 128:(j + 1) * 128], IDNF))
                tok = p.inc("te", fn, "TE") if j == 7 else (p.emit("te", fn) or 0)
            tp_tok[(tt, hc)] = tok
            p.wait("se", "TE", tok)
            if tt == 0 and hc == 0:
                p.wait("se", "VE", attn_ve)
                p.wait("se", "TE", attn_te)
            ct = p.inc("se", lambda s_, ps=ps, hc=hc, tt=tt: s_.copy(
                out=X1T.rearrange("p (hh q) -> p hh q", q=1024)[:, hc * 8:hc * 8 + 8,
                    tt * 128:(tt + 1) * 128],
                in_=ps.rearrange("p (a b) -> p a b", b=128)), "SE")
            copy_tok[(tt, hc)] = ct
            lx_free[slot] = ("SE", ct)
    xT_done = copy_tok[(OWN - 1, 1)]

    wo_free = [tk1["p1_te"], tk1["p1_te"]]
    add_tok = [0] * 16
    for h2 in range(16):
        slot = h2 % 2
        p.wait("gp", "TE", wo_free[slot])
        wl = p.dma("gp", WSL[slot].rearrange("p (dh q) -> p dh q", q=128), wo_r[h2],
                   sem=f"WL{slot}")
        ps = PS[h2 % 2]
        p.wait("te", f"WL{slot}", wl)
        if h2 == 0:
            p.wait("te", "VE", tk2["last_evac"])
        if h2 >= 2:
            p.wait("te", "VE", add_tok[h2 - 2])
        tok = 0
        for dh in range(16):
            for tc in range(2):
                fn = (lambda t_, dh=dh, tc=tc, ps=ps, sl=slot: nc.tensor.matmul(
                    ps[:, tc * 512:tc * 512 + 512],
                    WSL[sl][:, dh * 128:(dh + 1) * 128],
                    OT[:, dh * 1024 + tc * 512: dh * 1024 + tc * 512 + 512],
                    start=(dh == 0), stop=(dh == 15)))
                tok = p.inc("te", fn, "TE") if (dh == 15 and tc == 1) else (p.emit("te", fn) or tok)
        wo_free[slot] = tok
        p.wait("ve", "TE", tok)
        p.wait("ve", "SE", xT_done)
        add_tok[h2] = p.inc("ve", lambda v, ps=ps, h2=h2: v.tensor_tensor(
            out=X1T[:, h2 * 1024:(h2 + 1) * 1024],
            in0=X1T[:, h2 * 1024:(h2 + 1) * 1024], in1=ps[:], op=ALU.add), "VE")
    x1_done = add_tok[15]
    st["taps_avail"]["X1T"] = (X1T, F32, "VE", p.cnt["VE"])

    sq_tok = [0] * 16
    for hh in range(16):
        p.wait("se", "VE", add_tok[hh])
        sq_tok[hh] = p.inc("se", lambda s_, hh=hh: s_.activation(
            XG[:, hh * 1024:(hh + 1) * 1024], X1T[:, hh * 1024:(hh + 1) * 1024],
            AF.Square), "SE")
    tokm = tokq = 0
    for hh in range(16):
        p.wait("te", "VE", add_tok[hh])
        p.wait("te", "SE", sq_tok[hh])
        for tc in range(2):
            fnm = (lambda t_, hh=hh, tc=tc: nc.tensor.matmul(
                PS[2][:, tc * 512:tc * 512 + 512], ONESF,
                X1T[:, hh * 1024 + tc * 512: hh * 1024 + tc * 512 + 512],
                start=(hh == 0), stop=(hh == 15)))
            fnq = (lambda t_, hh=hh, tc=tc: nc.tensor.matmul(
                PS[3][:, tc * 512:tc * 512 + 512], ONESBF,
                XG[:, hh * 1024 + tc * 512: hh * 1024 + tc * 512 + 512],
                start=(hh == 0), stop=(hh == 15)))
            tokm = p.inc("te", fnm, "TE")
            tokq = p.inc("te", fnq, "TE")
    p.wait("ve", "TE", tokq)
    t1 = p.inc("ve", lambda v: v.tensor_scalar(out=MS, in0=PS[2][:], scalar1=1.0 / D,
               scalar2=None, op0=ALU.mult), "VE")
    p.wait("ve", "VE", t1)
    t2 = p.inc("ve", lambda v: v.tensor_tensor(out=VTMP, in0=MS, in1=MS, op=ALU.mult), "VE")
    p.wait("ve", "VE", t2)
    t3 = p.inc("ve", lambda v: v.scalar_tensor_tensor(
        out=VTMP, in0=PS[3][:], scalar=1.0 / D, in1=VTMP,
        op0=ALU.mult, op1=ALU.subtract), "VE")
    p.wait("se", "VE", t3)
    s1 = p.inc("se", lambda s_: s_.activation(RSTD, VTMP, AF.Ln, bias=SCR[:, 62:63]), "SE")
    p.wait("se", "SE", s1)
    s2 = p.inc("se", lambda s_: s_.activation(RSTD, RSTD, AF.Exp, scale=-0.5), "SE")
    p.wait("se", "SE", s2)
    s3 = p.inc("se", lambda s_: s_.mul(out=NEGMR[0:1, :], in_=MS[0:1, :], mul=-1.0), "SE")
    xln2_tok = [0] * 16
    prev_x2v = 0
    for hh in range(16):
        p.wait("ve", "SE", s2)
        p.wait("ve", "VE", t3)
        p.wait("ve", "VE", prev_x2v)
        a1 = p.inc("ve", lambda v, hh=hh: v.tensor_tensor(
            out=VTMP, in0=X1T[:, hh * 1024:(hh + 1) * 1024], in1=MS, op=ALU.subtract), "VE")
        p.wait("ve", "VE", a1)
        xln2_tok[hh] = p.inc("ve", lambda v, hh=hh: v.tensor_tensor(
            out=XLN2[:, hh * 1024:(hh + 1) * 1024], in0=VTMP, in1=RSTD, op=ALU.mult), "VE")
        prev_x2v = xln2_tok[hh]
    st["taps_avail"]["XLN2"] = (XLN2, BF16, "VE", p.cnt["VE"])

    p.wait("sy", "TE", tp_tok[(OWN - 1, 1)])
    gwt_ld = p.dma("sy", GWT, dr["gate_wT"].rearrange("(hh p) e -> p hh e", p=128))
    p.wait("te", "LD", gwt_ld)
    p.wait("te", "VE", t3)
    ctok = 0
    for hh in range(16):
        fn = (lambda t_, hh=hh: nc.tensor.matmul(
            PS[1][:, 0:64], ONESF, GWT[:, hh * 64:(hh + 1) * 64],
            start=(hh == 0), stop=(hh == 15)))
        ctok = p.inc("te", fn, "TE") if hh == 15 else (p.emit("te", fn) or ctok)
    p.wait("se", "TE", ctok)
    cs_tok = p.inc("se", lambda s_: s_.copy(out=SCR[0:1, 0:64], in_=PS[1][0:1, 0:64]), "SE")
    rtok = 0
    p.wait("te", "SE", s2)
    for tt in range(8):
        fn = (lambda t_, tt=tt: nc.tensor.matmul(
            PS[1][:, 512 + tt:512 + tt + 1], RSTD[0:1, tt * 128:(tt + 1) * 128],
            ONESROW[0:1, 0:1], start=True, stop=True))
        rtok = p.inc("te", fn, "TE")
    p.wait("se", "TE", rtok)
    rc_tok = p.inc("se", lambda s_: s_.copy(out=RCOLS, in_=PS[1][:, 512:520]), "SE")
    lg_tok = 0
    p.wait("te", "SE", s3)
    p.wait("te", "VE", add_tok[15])
    for tt in range(8):
        for hh in range(16):
            fn = (lambda t_, tt=tt, hh=hh: nc.tensor.matmul(
                PS[0][:, tt * 64:(tt + 1) * 64],
                X1T[:, hh * 1024 + tt * 128: hh * 1024 + tt * 128 + 128],
                GWT[:, hh * 64:(hh + 1) * 64], start=(hh == 0), stop=False))
            p.emit("te", fn)
        p.wait("te", "SE", cs_tok)
        lg_tok = p.inc("te", lambda t_, tt=tt: nc.tensor.matmul(
            PS[0][:, tt * 64:(tt + 1) * 64], NEGMR[0:1, tt * 128:(tt + 1) * 128],
            SCR[0:1, 0:64], start=False, stop=True), "TE")
    LG = f32v(149504, 512)
    p.wait("ve", "TE", lg_tok)
    p.wait("ve", "SE", rc_tok)
    lgs_tok = 0
    for tt in range(8):
        lgs_tok = p.inc("ve", lambda v, tt=tt: v.tensor_scalar(
            out=LG[:, tt * 64:(tt + 1) * 64], in0=PS[0][:, tt * 64:(tt + 1) * 64],
            scalar1=RCOLS[:, tt:tt + 1], scalar2=None, op0=ALU.mult), "VE")
    WE = f32v(151552, 512)
    we_tok = [0] * 8
    for tt in range(8):
        p.wait("se", "VE", lgs_tok)
        e1 = p.inc("se", lambda s_, tt=tt: s_.activation(
            ES[:, tt * 64:(tt + 1) * 64], LG[:, tt * 64:(tt + 1) * 64], AF.Exp,
            accum_out=SCR[:, 16 + tt:17 + tt]), "SE")
        p.wait("ve", "SE", e1)
        m1 = p.inc("ve", lambda v, tt=tt: v.reciprocal(
            out=SCR[:, 24 + tt:25 + tt], in_=SCR[:, 16 + tt:17 + tt]), "VE")
        p.wait("ve", "VE", m1)
        m2 = p.inc("ve", lambda v, tt=tt: v.max(
            out=SCR[:, 32:40], in_=ES[:, tt * 64:(tt + 1) * 64]), "VE")
        p.wait("ve", "VE", m2)
        p.wait("ve", "VE", prev_x2v)
        m3 = p.inc("ve", lambda v, tt=tt: v.tensor_scalar(
            out=VTMP[:, 0:64], in0=ES[:, tt * 64:(tt + 1) * 64],
            scalar1=SCR[:, 39:40], scalar2=None, op0=ALU.is_ge), "VE")
        p.wait("ve", "VE", m3)
        m4 = p.inc("ve", lambda v, tt=tt: v.tensor_tensor(
            out=VTMP[:, 0:64], in0=VTMP[:, 0:64],
            in1=ES[:, tt * 64:(tt + 1) * 64], op=ALU.mult), "VE")
        p.wait("ve", "VE", m4)
        we_tok[tt] = p.inc("ve", lambda v, tt=tt: v.tensor_scalar(
            out=WE[:, tt * 64:(tt + 1) * 64], in0=VTMP[:, 0:64],
            scalar1=SCR[:, 24 + tt:25 + tt], scalar2=None, op0=ALU.mult), "VE")
        prev_x2v = we_tok[tt]
    wt_tok = 0
    for tt in range(8):
        p.wait("te", "VE", we_tok[tt])
        fn = (lambda t_, tt=tt: nc.tensor.transpose(
            PS[2][0:64, tt * 128:(tt + 1) * 128].  __getitem__((slice(None), slice(0, 128))) if False else
            PS[2][0:64, tt * 128:(tt + 1) * 128],
            WE[:, tt * 64:(tt + 1) * 64], IDNF))
        wt_tok = p.inc("te", fn, "TE")
    p.wait("se", "TE", wt_tok)
    wet_tok = p.inc("se", lambda s_: s_.copy(out=WET[0:64, :], in_=PS[2][0:64, 0:1024]), "SE")
    p.wait("te", "WL", st["wl_inv"])
    ib_tok = p.inc("te", lambda t_: nc.tensor.matmul(
        PS[3][0:64, 0:64], ONESROW[0:1, 0:64], INVF[0:1, 0:64],
        start=True, stop=True), "TE")
    p.wait("ve", "TE", ib_tok)
    pm_tok = p.inc("ve", lambda v: v.tensor_scalar(
        out=PERM[0:64, :], in0=PS[3][0:64, 0:64], scalar1=IOTA64[0:64, :],
        scalar2=None, op0=ALU.is_equal), "VE")
    p.wait("te", "VE", pm_tok)
    p.wait("te", "SE", wet_tok)
    ft_tok = 0
    for tc in range(2):
        fn = (lambda t_, tc=tc: nc.tensor.matmul(
            PS[3][0:64, tc * 512:(tc + 1) * 512], PERM[0:64, 0:64],
            WET[0:64, tc * 512:(tc + 1) * 512], start=True, stop=True))
        ft_tok = p.inc("te", fn, "TE")
    p.wait("se", "TE", ft_tok)
    flt_tok = p.inc("se", lambda s_: s_.copy(out=FLT[0:64, :], in_=PS[3][0:64, 0:1024]), "SE")
    p.wait("ve", "SE", flt_tok)
    mt_tok = p.inc("ve", lambda v: v.tensor_scalar(
        out=WET[0:64, :], in0=FLT[0:64, :], scalar1=0.0, scalar2=None,
        op0=ALU.is_gt), "VE")
    p.wait("se", "SE", flt_tok)
    p.wait("se", "VE", prev_x2v)
    ez_tok = p.inc("se", lambda s_: s_.activation(VTMP[0:64, :], FLT[0:64, :], AF.Exp), "SE")
    p.wait("ve", "SE", ez_tok)
    p.wait("ve", "VE", mt_tok)
    nm_tok = p.inc("ve", lambda v: v.tensor_tensor(
        out=NUMT[0:64, :], in0=VTMP[0:64, :], in1=WET[0:64, :], op=ALU.mult), "VE")
    p.wait("te", "VE", nm_tok)
    dn_tok = 0
    for tc in range(2):
        fn = (lambda t_, tc=tc: nc.tensor.matmul(
            PS[2][0:64, tc * 512:(tc + 1) * 512], SGRP[0:64, :],
            NUMT[0:64, tc * 512:(tc + 1) * 512], start=True, stop=True))
        dn_tok = p.inc("te", fn, "TE")
    p.wait("ve", "TE", dn_tok)
    p.wait("ve", "VE", nm_tok)
    d1 = p.inc("ve", lambda v: v.tensor_scalar(
        out=VTMP[0:64, :], in0=PS[2][0:64, 0:1024], scalar1=1e-30, scalar2=None,
        op0=ALU.add), "VE")
    p.wait("ve", "VE", d1)
    d2 = p.inc("ve", lambda v: v.reciprocal(out=RZ[0:64, :], in_=VTMP[0:64, :]), "VE")
    p.wait("ve", "VE", d2)
    p.wait("ve", "VE", nm_tok)
    soft_tok = p.inc("ve", lambda v: v.tensor_tensor(
        out=SOFTT[0:64, :], in0=NUMT[0:64, :], in1=RZ[0:64, :], op=ALU.mult), "VE")
    p.wait("ve", "VE", soft_tok)
    p.wait("ve", "VE", mt_tok)
    wt2_tok = p.inc("ve", lambda v: v.tensor_tensor(
        out=WT2[0:64, :], in0=FLT[0:64, :], in1=WET[0:64, :], op=ALU.mult), "VE")
    st["taps_avail"]["SOFTT"] = (SOFTT, BF16, "VE", soft_tok)
    st["taps_avail"]["WT2"] = (WT2, F32, "VE", wt2_tok)
    st["p3_tokens"] = dict(x1_done=x1_done, xln2_tok=xln2_tok, soft_tok=soft_tok,
                           wt2_tok=wt2_tok, p3_te=p.cnt["TE"], p3_ve=p.cnt["VE"],
                           p3_se=p.cnt["SE"])


def build_p4(st):
    p = st["p"]; nc = st["nc"]; dr = st["dr"]; PS = st["PS"]
    X1T = st["X1T"]; XLN2 = st["XLN2"]; XG = st["XG"]; APR = st["APR"]
    SOFTT = st["SOFTT"]; SOFTG = st["SOFTG"]; GMASK = st["SG8"]
    WT2 = st["WT2"]; GGRP = st["GGRP"]; MUTA = st["MUTA"]; SCBC = st["SCBC"]
    IDNF = st["IDNF"]; bfv = st["bfv"]; f32v = st["f32v"]
    out = st["out"]
    tk3 = st["p3_tokens"]

    SIL = bfv(188416, 1024)
    GSL = [bfv(172032, 2048), bfv(176128, 2048)]
    USL = [bfv(180224, 2048), bfv(184320, 2048)]
    DSL = [bfv(153600, 11 * 128), bfv(157696, 11 * 128)]
    OUTB = [f32v(98304, 2048), f32v(106496, 2048),
            f32v(114688, 2048), f32v(122880, 2048)]

    for s in ("WG0", "WG1", "WU0", "WU1", "WD0", "WD1"):
        if s not in p.sem:
            p.sem[s] = nc.alloc_semaphore(s); p.cnt[s] = 0

    gateT_r = dr["gateT"].rearrange("g (hh p) (it q) -> g it p hh q", p=128, q=128)
    upT_r = dr["upT"].rearrange("g (hh p) (it q) -> g it p hh q", p=128, q=128)
    downT_r = dr["downT"].rearrange("g (it p) (hh q) -> g hh p it q", p=128, q=128)

    p.wait("sy", "TE", tk3["p3_te"])
    ggrp_ld = p.dma("sy", GGRP[0:64, :], dr["Ggrp"])
    p.wait("gp", "TE", tk3["p3_te"])
    muta_ld = p.dma("gp", MUTA[0:64, :], dr["mutaT"])

    gate_free = [tk3["p3_te"], tk3["p3_te"]]
    up_free = [tk3["p3_te"], tk3["p3_te"]]
    down_free = [tk3["p3_te"], tk3["p3_te"]]
    gate_ld = {}; up_ld = {}; down_ld = {}

    def fetch_gate(g, i):
        slot = (g * NI + i) % 2
        p.wait("gp", "TE", gate_free[slot])
        gate_ld[(g, i)] = (f"WG{slot}", p.dma(
            "gp", GSL[slot].rearrange("p (hh q) -> p hh q", q=128),
            gateT_r[g, i], sem=f"WG{slot}"), slot)

    def fetch_up(g, i):
        slot = (g * NI + i) % 2
        p.wait("gp", "TE", up_free[slot])
        up_ld[(g, i)] = (f"WU{slot}", p.dma(
            "gp", USL[slot].rearrange("p (hh q) -> p hh q", q=128),
            upT_r[g, i], sem=f"WU{slot}"), slot)

    def fetch_down(g, hh):
        slot = (g * 16 + hh) % 2
        p.wait("gp", "TE", down_free[slot])
        down_ld[(g, hh)] = (f"WD{slot}", p.dma(
            "gp", DSL[slot].rearrange("p (it q) -> p it q", q=128),
            downT_r[g, hh], sem=f"WD{slot}"), slot)

    p.wait("gp", "VE", tk3["wt2_tok"])
    p.wait("gp", "SE", tk3["p3_se"])
    fetch_gate(0, 0); fetch_up(0, 0)
    fetch_gate(0, 1); fetch_up(0, 1)
    fetch_down(0, 0); fetch_down(0, 1)

    xg_add = [0] * 16
    x2_add = [0] * 16
    apr_tok = [0] * NI
    scbc_ve = 0
    softg_ve = 0
    state = dict(scbc_ve=0, softg_ve=0)

    def emit_scbc_softg(g):
        p.wait("te", "VE", tk3["wt2_tok"] if g == 0 else apr_tok[NI - 1])
        if g == 0:
            p.wait("te", "LD", ggrp_ld)
        sc_tok = 0
        for tc in range(2):
            fn = (lambda t_, g=g, tc=tc: nc.tensor.matmul(
                PS[0][:, tc * 512:(tc + 1) * 512], GGRP[0:64, g * 128:(g + 1) * 128],
                WT2[0:64, tc * 512:(tc + 1) * 512], start=True, stop=True))
            sc_tok = p.inc("te", fn, "TE")
        p.wait("ve", "TE", sc_tok)
        p.wait("ve", "VE", state["scbc_ve"])
        state["scbc_ve"] = p.inc("ve", lambda v: v.tensor_copy(out=SCBC, in_=PS[0][:]), "VE")
        p.wait("ve", "VE", tk3["soft_tok"] if g == 0 else state["softg_ve"])
        state["softg_ve"] = p.inc("ve", lambda v, g=g: v.tensor_scalar(
            out=SOFTG[0:64, :], in0=SOFTT[0:64, :], scalar1=GMASK[0:64, g:g + 1],
            scalar2=None, op0=ALU.mult), "VE")

    def emit_xg_hh(g, hh):
        psX = PS[0] if hh % 2 == 0 else PS[3]
        if hh >= 2:
            p.wait("te", "VE", xg_add[hh - 2])
        elif hh == 0:
            p.wait("te", "VE", state["scbc_ve"])
            p.wait("te", "VE", state["softg_ve"])
            if g == 0:
                p.wait("te", "WL", muta_ld)
        tok = 0
        for tc in range(2):
            fn = (lambda t_, hh=hh, tc=tc, psX=psX: nc.tensor.matmul(
                psX[:, tc * 512:(tc + 1) * 512], MUTA[0:64, hh * 128:(hh + 1) * 128],
                SOFTG[0:64, tc * 512:(tc + 1) * 512], start=True, stop=True))
            tok = p.inc("te", fn, "TE")
        p.wait("ve", "TE", tok)
        p.wait("ve", "VE", xg_add[hh])
        p.wait("ve", "VE", tk3["xln2_tok"][hh])
        xg_add[hh] = p.inc("ve", lambda v, hh=hh, psX=psX: v.tensor_tensor(
            out=XG[:, hh * 1024:(hh + 1) * 1024], in0=psX[:],
            in1=XLN2[:, hh * 1024:(hh + 1) * 1024], op=ALU.add), "VE")

    prev_sil_tok = 0
    prev_sil_se = 0
    for g in range(G):
        if g == 0:
            emit_scbc_softg(0)
            for hh in range(16):
                emit_xg_hh(0, hh)
        for i in range(NI):
            kg, vg, sg = gate_ld[(g, i)]
            ku, vu, su = up_ld[(g, i)]
            psG = PS[0] if i % 2 == 0 else PS[2]
            psU = PS[1] if i % 2 == 0 else PS[3]
            p.wait("te", kg, vg)
            p.wait("te", ku, vu)
            if i >= 2:
                p.wait("te", "VE", apr_tok[i - 2])
            elif i == 0:
                p.wait("te", "VE", xg_add[14])
                if g > 0:
                    p.wait("te", "VE", x2_add[14])
            elif i == 1:
                if g > 0:
                    p.wait("te", "VE", x2_add[15])
                p.wait("te", "VE", xg_add[15])
            tok = 0
            for hh in range(16):
                if hh == 0:
                    p.wait("te", "VE", xg_add[15])
                for tc in range(2):
                    fn = (lambda t_, hh=hh, tc=tc, psG=psG, sl=sg: nc.tensor.matmul(
                        psG[:, tc * 512:(tc + 1) * 512], GSL[sl][:, hh * 128:(hh + 1) * 128],
                        XG[:, hh * 1024 + tc * 512: hh * 1024 + tc * 512 + 512],
                        start=(hh == 0), stop=(hh == 15)))
                    p.emit("te", fn)
                for tc in range(2):
                    fn = (lambda t_, hh=hh, tc=tc, psU=psU, sl=su: nc.tensor.matmul(
                        psU[:, tc * 512:(tc + 1) * 512], USL[sl][:, hh * 128:(hh + 1) * 128],
                        XG[:, hh * 1024 + tc * 512: hh * 1024 + tc * 512 + 512],
                        start=(hh == 0), stop=(hh == 15)))
                    tok = p.inc("te", fn, "TE") if (hh == 15 and tc == 1) else (p.emit("te", fn) or tok)
            gate_free[sg] = tok
            up_free[su] = tok
            if i + 2 < NI:
                fetch_gate(g, i + 2); fetch_up(g, i + 2)
            elif g + 1 < G and i + 2 - NI < 2:
                fetch_gate(g + 1, i + 2 - NI); fetch_up(g + 1, i + 2 - NI)
            p.wait("se", "TE", tok)
            p.wait("se", "VE", prev_sil_tok)
            p.wait("se", "SE", prev_sil_se)
            sil_tok = p.inc("se", lambda s_, psG=psG: s_.activation(
                SIL, psG[:], AF.Silu), "SE")
            prev_sil_se = sil_tok
            p.wait("ve", "SE", sil_tok)
            p.wait("ve", "VE", apr_tok[i])
            a1 = p.inc("ve", lambda v, i=i, psU=psU: v.tensor_tensor(
                out=APR[:, i * 1024:(i + 1) * 1024], in0=psU[:], in1=SIL, op=ALU.mult), "VE")
            prev_sil_tok = a1
            p.wait("ve", "VE", a1)
            p.wait("ve", "VE", state["scbc_ve"])
            apr_tok[i] = p.inc("ve", lambda v, i=i: v.tensor_tensor(
                out=APR[:, i * 1024:(i + 1) * 1024], in0=APR[:, i * 1024:(i + 1) * 1024],
                in1=SCBC, op=ALU.mult), "VE")
        if g + 1 < G:
            emit_scbc_softg(g + 1)
        for hh in range(16):
            kd, vd, sd = down_ld[(g, hh)]
            psD = PS[1] if hh % 2 == 0 else PS[2]
            p.wait("te", kd, vd)
            if hh >= 2:
                p.wait("te", "VE", x2_add[hh - 2])
            elif hh == 0:
                p.wait("te", "VE", apr_tok[NI - 2])
                p.wait("te", "VE", apr_tok[NI - 1])
            elif hh == 1:
                p.wait("te", "VE", apr_tok[NI - 1])
            tok = 0
            for it in range(NI):
                if it == 0 and hh == 0:
                    p.wait("te", "VE", apr_tok[NI - 1])
                for tc in range(2):
                    fn = (lambda t_, it=it, tc=tc, psD=psD, sl=sd: nc.tensor.matmul(
                        psD[:, tc * 512:(tc + 1) * 512], DSL[sl][:, it * 128:(it + 1) * 128],
                        APR[:, it * 1024 + tc * 512: it * 1024 + tc * 512 + 512],
                        start=(it == 0), stop=(it == NI - 1)))
                    tok = p.inc("te", fn, "TE") if (it == NI - 1 and tc == 1) else (p.emit("te", fn) or tok)
            down_free[sd] = tok
            if hh + 2 < 16:
                fetch_down(g, hh + 2)
            elif g + 1 < G:
                fetch_down(g + 1, hh + 2 - 16)
            p.wait("ve", "TE", tok)
            p.wait("ve", "VE", x2_add[hh])
            x2_add[hh] = p.inc("ve", lambda v, hh=hh, psD=psD: v.tensor_tensor(
                out=X1T[:, hh * 1024:(hh + 1) * 1024],
                in0=X1T[:, hh * 1024:(hh + 1) * 1024], in1=psD[:], op=ALU.add), "VE")
            if g + 1 < G:
                if hh < 14:
                    emit_xg_hh(g + 1, hh)
                    if hh == 13:
                        emit_xg_hh(g + 1, 14)
                        emit_xg_hh(g + 1, 15)

    o_r = out.rearrange("(tt p) d -> tt p d", p=128)
    ob_free = [("VE", 0)] * 4
    for tt in range(8):
        par = tt % 4
        ps_pair = (PS[0], PS[3])
        if tt == 0:
            p.wait("te", "VE", apr_tok[NI - 1])
            p.wait("ve", "VE", max(xg_add))
        else:
            p.wait("te", "VE", prev_e2)
        if tt >= 4:
            p.wait("ve", *ob_free[par])
        tok = 0
        for hh in range(16):
            if tt == 0:
                p.wait("te", "VE", x2_add[hh])
            ps = ps_pair[hh // 8]
            fn = (lambda t_, hh=hh, tt=tt, ps=ps: nc.tensor.transpose(
                ps[:, (hh % 8) * 128:(hh % 8 + 1) * 128],
                X1T[:, hh * 1024 + tt * 128: hh * 1024 + tt * 128 + 128], IDNF))
            tok = p.inc("te", fn, "TE") if hh == 15 else (p.emit("te", fn) or tok)
        p.wait("ve", "TE", tok)
        e1 = p.inc("ve", lambda v, par=par: v.tensor_copy(
            out=OUTB[par][:, 0:1024], in_=PS[0][:]), "VE")
        e2 = p.inc("ve", lambda v, par=par: v.tensor_copy(
            out=OUTB[par][:, 1024:2048], in_=PS[3][:]), "VE")
        ob_free[par] = ("VE", e2)
        prev_e2 = e2
        p.wait("sy", "VE", e2)
        od = p.outdma(o_r[tt], OUTB[par], sem=f"OU{par}")
        ob_free[par] = (f"OU{par}", od)
    st["taps_avail"]["XG"] = (XG, BF16, "VE", p.cnt["VE"])
    st["taps_avail"]["APR"] = (APR, BF16, "VE", p.cnt["VE"])
    st["taps_avail"]["SCBC"] = (SCBC, F32, "VE", p.cnt["VE"])
    st["taps_avail"]["SOFTG"] = (SOFTG, BF16, "VE", p.cnt["VE"])
    st["p4_tokens"] = dict(done=True)


def kernel(**inputs):
    shared, per_core = stage_host(inputs)
    nc, _ = build(taps=(), upto=4)
    in_maps = []
    for c in range(8):
        m = dict(shared)
        pc = per_core[c]
        m.update({k: pc[k] for k in ("x_perm", "COS2_K", "SIN2_K", "COS2_Q",
                                     "SIN2_Q", "alibi_k", "alibiq", "clampk",
                                     "masks")})
        in_maps.append({k: np.asarray(v) for k, v in m.items()})
    res = run_bass_kernel_spmd(nc, in_maps, core_ids=list(range(8)))
    outp = np.empty((B, T, D), np.float32)
    for c in range(8):
        o = res.results[c]["out"].reshape(OWN, 128, D)
        for j, pb in enumerate(per_core[c]["own_blocks"]):
            outp[c // 2, pb * 128:(pb + 1) * 128, :] = o[j]
    return outp


# version 21
# speedup vs baseline: 1.1724x; 1.1724x over previous
import numpy as np
import ml_dtypes
import concourse.bass as bass
import concourse.mybir as mybir
from concourse.bass_utils import run_bass_kernel_spmd

try:
    from concourse import compiler_utils as _cu
    _flags = _cu.get_compiler_flags()
    _nf = [f.replace("--enable-ldw-opt=false", "--enable-ldw-opt=true")
           for f in _flags]
    if _nf != _flags:
        _cu.set_compiler_flags(_nf)
except Exception:
    pass

F32 = mybir.dt.float32
BF16 = mybir.dt.bfloat16
I32 = mybir.dt.int32
AF = mybir.ActivationFunctionType
ALU = mybir.AluOpType
AX = mybir.AxisListType
BF = ml_dtypes.bfloat16

B, T, D = 4, 2048, 2048
NH, NKV, HD = 16, 4, 128
G, I, E, GS = 8, 1408, 64, 8
NI = 11
OWN, TOWN = 8, 1024
C_SCALE = float(1.0 / np.sqrt(128.0))
EPS = 1e-5

def _attn_sched():
    sched = {0: [], 1: []}
    slot = 0
    for gq in (0, 1):
        base = 4 * gq
        for kp in range(0, base + 4):
            s = max(0, kp - base)
            sched[gq].append(("own", kp, s, slot)); slot += 1
        for kp in range(8, 8 + base + 4):
            s = max(0, (kp - 8) - base)
            sched[gq].append(("oth", kp, s, slot)); slot += 1
    return sched

SCHED = _attn_sched()
NSLOT = sum(len(v) for v in SCHED.values())
_SLOPES = 2.0 ** (-8.0 * np.arange(1, NH + 1) / NH)

def _active_tiles(h, gq):
    base = 4 * gq
    out = []
    for kind, kp, s, slot in SCHED[gq]:
        jlead = base + s
        if kind == "own":
            dmin = 256 * (jlead - kp) - 127
        else:
            dmin = 256 * (jlead - (kp - 8)) - 255
        if _SLOPES[h] * max(dmin, 0) < 150.0:
            out.append((kind, kp, s, slot))
    return out



def _f32(a):
    return np.ascontiguousarray(a, dtype=np.float32)


def stage_host(inputs):
    x = np.asarray(inputs["L0_attn_x"], dtype=np.float32)
    gate_w = _f32(inputs["L0_ffn_gate_w"])
    mask_up = _f32(inputs["L0_ffn_mask_up_w"])
    inv_map = np.asarray(inputs["L0_ffn_inv_map"]).astype(np.int32)

    muta = np.ascontiguousarray(mask_up.transpose(0, 2, 1).reshape(G * GS, D))
    gmask = np.zeros((64, 8), np.float32)
    for g in range(8):
        gmask[g * 8:(g + 1) * 8, g] = 1.0
    ggrp = np.zeros((64, 8 * 128), np.float32)
    for jp in range(64):
        ggrp[jp, (jp // 8) * 128:((jp // 8) + 1) * 128] = 1.0

    shared = {
        "Wq": _f32(inputs["L0_attn_Wq"]), "Wk": _f32(inputs["L0_attn_Wk"]),
        "Wv": _f32(inputs["L0_attn_Wv"]), "Wo": _f32(inputs["L0_attn_Wo"]),
        "gate_wT": _f32(gate_w.T),
        "mutaT": muta,
        "gateT": _f32(inputs["L0_ffn_gate_proj_w"].transpose(0, 2, 1)),
        "upT": _f32(inputs["L0_ffn_up_proj_w"].transpose(0, 2, 1)),
        "downT": _f32(inputs["L0_ffn_down_proj_w"].transpose(0, 2, 1)),
        "inv_flat": inv_map.reshape(1, E),
        "IDN_BF": np.eye(128, dtype=np.float32).astype(BF),
        "IDN_F32": np.eye(128, dtype=np.float32),
        "ONES_BF": np.ones((128, 128), np.float32).astype(BF),
        "ONES_F32": np.ones((128, 128), np.float32),
        "ones_row": np.ones((1, 128), np.float32),
        "iota64": np.arange(64, dtype=np.float32).reshape(64, 1),
        "Sgrp": np.kron(np.eye(8, dtype=np.float32), np.ones((8, 8), np.float32)).astype(BF),
        "Ggrp": ggrp.astype(BF), "Gmask": gmask.astype(BF), "Sg8": gmask.astype(np.float32),
        "JROT": np.block([[np.zeros((64, 64), np.float32), np.eye(64, dtype=np.float32)],
                          [-np.eye(64, dtype=np.float32), np.zeros((64, 64), np.float32)]]).astype(BF),
    }

    slopes = 2.0 ** (-8.0 * np.arange(1, NH + 1, dtype=np.float32) / NH)
    theta = 1.0 / 10000.0 ** (np.arange(64, dtype=np.float32) / 64.0)

    per_core = []
    for c in range(8):
        b, half = c // 2, c % 2
        own_blocks = [half + 2 * j for j in range(OWN)]
        oth_blocks = [(1 - half) + 2 * j for j in range(OWN)]
        perm_rows = np.concatenate(
            [np.arange(pb * 128, (pb + 1) * 128) for pb in own_blocks + oth_blocks])
        pos = perm_rows.astype(np.float32)
        ang = np.outer(pos, theta)
        cosk = np.cos(ang).T; sink = np.sin(ang).T
        alibi_k = np.zeros((128, 256), np.float32)
        for h in range(NH):
            for kp in range(16):
                alibi_k[:, h * 16 + kp] = slopes[h] * pos[kp * 128:(kp + 1) * 128]
        alibiq = (-slopes[:, None] * pos[None, :TOWN] * np.sqrt(128.0)).astype(BF)
        clampk = np.zeros((128, 256), np.float32)
        for h in range(NH):
            for kp in range(16):
                clampk[:, h * 16 + kp] = (85.0 - slopes[h] * pos[kp * 128:(kp + 1) * 128]) * np.sqrt(128.0)

        tri = (np.arange(128)[:, None] <= np.arange(128)[None, :]).astype(np.float32)
        masks = np.ones((128, NSLOT * 128), np.float32)
        for gq in (0, 1):
            for kind, kp, s, slot in SCHED[gq]:
                jlead = 4 * gq + s
                if kind == "own" and kp == jlead:
                    masks[:, slot * 128:(slot + 1) * 128] = tri
                elif kind == "oth" and (kp - 8) == jlead and half == 0:
                    masks[:, slot * 128:(slot + 1) * 128] = 0.0

        per_core.append({
            "x_perm": _f32(x[b][perm_rows]),
            "COS2_K": _f32(np.vstack([cosk, cosk])),
            "SIN2_K": _f32(np.vstack([sink, sink])),
            "COS2_Q": _f32(np.vstack([cosk[:, :TOWN], cosk[:, :TOWN]])),
            "SIN2_Q": _f32(np.vstack([sink[:, :TOWN], sink[:, :TOWN]])),
            "alibi_k": alibi_k, "alibiq": alibiq, "clampk": clampk,
            "masks": masks.astype(BF),
            "own_blocks": own_blocks,
        })
    return shared, per_core


class Prog:
    def __init__(self, nc):
        self.nc = nc
        self.ops = {k: [] for k in ("te", "ve", "se", "gp", "sy")}
        self.sem = {}
        self.cnt = {}
        for name in ("LD", "WL", "LX0", "LX1", "LX2", "LX3", "LX4", "LX5",
                     "WL0", "WL1", "WK0", "WK1",
                     "TE", "VE", "SE", "OUT", "OU0", "OU1", "OU2", "OU3"):
            self.sem[name] = nc.alloc_semaphore(name)
            self.cnt[name] = 0

    def emit(self, eng, fn):
        self.ops[eng].append(fn)

    def inc(self, eng, fn, sem, n=1):
        self.cnt[sem] += n
        h = self.sem[sem]
        self.ops[eng].append(lambda e, fn=fn, h=h, n=n: fn(e).then_inc(h, n))
        return self.cnt[sem]

    def wait(self, eng, sem, val):
        if val <= 0:
            return
        h = self.sem[sem]
        self.ops[eng].append(lambda e, h=h, v=val: e.wait_ge(h, v))

    def dma(self, q, out, in_, sem=None):
        if sem is None:
            sem = "LD" if q == "sy" else "WL"
        if sem not in ("LD", "OUT") and self.cnt.get(sem, 0) > 0:
            self.wait(q, sem, self.cnt[sem])
        return self.inc(q, lambda e, o=out, i=in_: e.dma_start(out=o, in_=i), sem, 16)

    def outdma(self, out, in_, sem="OUT"):
        if sem != "OUT" and self.cnt.get(sem, 0) > 0:
            self.wait("sy", sem, self.cnt[sem])
        return self.inc("sy", lambda e, o=out, i=in_: e.dma_start(out=o, in_=i), sem, 16)


def build(taps=(), upto=4, sub=99):
    nc = bass.Bass("TRN2", target_bir_lowering=False)
    dr = {}
    def din(name, shape, dtype):
        dr[name] = nc.dram_tensor(name, list(shape), dtype, kind="ExternalInput").ap()

    din("x_perm", (T, D), F32)
    din("Wq", (D, D), F32); din("Wk", (D, 512), F32); din("Wv", (D, 512), F32)
    din("Wo", (D, D), F32)
    din("gate_wT", (D, E), F32); din("mutaT", (64, D), F32)
    din("gateT", (G, D, I), F32); din("upT", (G, D, I), F32); din("downT", (G, I, D), F32)
    din("inv_flat", (1, E), I32)
    din("IDN_BF", (128, 128), BF16); din("IDN_F32", (128, 128), F32)
    din("ONES_BF", (128, 128), BF16); din("ONES_F32", (128, 128), F32)
    din("ones_row", (1, 128), F32); din("iota64", (64, 1), F32)
    din("Sgrp", (64, 64), BF16); din("Ggrp", (64, 8 * 128), BF16); din("Gmask", (64, 8), BF16)
    din("Sg8", (64, 8), F32)
    din("COS2_K", (128, T), F32); din("SIN2_K", (128, T), F32)
    din("COS2_Q", (128, TOWN), F32); din("SIN2_Q", (128, TOWN), F32)
    din("alibi_k", (128, 256), F32); din("alibiq", (NH, TOWN), BF16)
    din("clampk", (128, 256), F32)
    din("JROT", (128, 128), BF16)
    din("masks", (128, NSLOT * 128), BF16)
    out = nc.dram_tensor("out", [TOWN, D], F32, kind="ExternalOutput").ap()

    AR = nc.alloc_sbuf_tensor("AR", [128, 212352], mybir.dt.uint8)
    ARB = AR.ap().bitcast(BF16)
    ARF = AR.ap().bitcast(F32)
    def bfv(off_b, n):
        assert off_b % 2 == 0
        return ARB[:, off_b // 2: off_b // 2 + n]
    def f32v(off_b, n):
        assert off_b % 4 == 0
        return ARF[:, off_b // 4: off_b // 4 + n]

    XLN  = bfv(0, 16 * 2048)
    MASKS = bfv(0, NSLOT * 128)
    ALQ  = [bfv(8192, 1024), bfv(12288, 1024)]
    X1T  = f32v(0, 16 * 1024)
    QT   = bfv(65536, 16 * 1024)
    XG   = bfv(65536, 16 * 1024)
    COS2Q = f32v(98304, 1024); SIN2Q = f32v(102400, 1024)
    COS2K = f32v(106496, 2048); SIN2K = f32v(114688, 2048)
    OT   = bfv(98304, 16 * 1024)
    XLN2 = bfv(98304, 16 * 1024)
    KT   = bfv(131072, 4 * 2048)
    VTOK = bfv(147456, 4 * 2048)
    APR  = bfv(131072, NI * 1024)
    WET  = bfv(153600, 1024)
    FLT  = bfv(157696, 1024)
    SOFTT = bfv(161792, 1024)
    XL   = [f32v(163840, 1024), f32v(167936, 1024),
            f32v(65536, 1024), f32v(69632, 1024)]
    NUMT = bfv(163840, 1024)
    GWT  = f32v(167936, 1024)
    SCBC = f32v(167936, 1024)
    RING = 172032
    WKV  = [bfv(172032, 2048), bfv(176128, 2048)]
    VTTF = f32v(180224, 2048)
    RAW = bfv(188416, 1024)
    VJUNK = f32v(180224, 1024)
    HJUNK = f32v(188416, 1024)
    NEGMR = f32v(188416, 1024)
    ATMP = f32v(188416, 1024)
    WSL  = [bfv(192512, 2048), bfv(196608, 2048)]
    XLNORMP = [f32v(192512, 1024), f32v(196608, 1024)]
    GGRP = bfv(192512, 1024)
    MUTA = bfv(196608, 2048)
    co = 200704
    def cst(nbytes):
        nonlocal co
        o = co; co += (nbytes + 63) & ~63
        return o
    IDNBF = bfv(cst(256), 128)
    JROT = bfv(cst(256), 128)
    IDNF  = f32v(cst(512), 128)
    ONESBF = bfv(cst(256), 128)
    ONESF  = f32v(cst(512), 128)
    ONESROW = f32v(cst(512), 128)
    ALIBIK = f32v(cst(1024), 256)
    CLAMPK = f32v(cst(1024), 256)
    IOTA64 = f32v(cst(64), 1)
    SGRP = bfv(cst(128), 64)
    PERM = bfv(cst(128), 64)
    INVF = f32v(cst(256), 64)
    GMASK = bfv(cst(64), 8)
    SOFTG = bfv(cst(2048), 1024)
    RCOLS = f32v(cst(64), 8)
    SG8 = f32v(cst(64), 8)
    SCR  = f32v(cst(256), 64)
    assert co <= 212352, co

    PS = [nc.alloc_psum_tensor(f"PS{k}", [128, 1024], F32).ap() for k in range(4)]

    p = Prog(nc)
    taps_avail = {}

    for s_, d_ in [(IDNBF, dr["IDN_BF"]), (IDNF, dr["IDN_F32"]), (JROT, dr["JROT"]),
                   (ONESBF, dr["ONES_BF"]), (ONESF, dr["ONES_F32"]),
                   (ONESROW[0:1, :], dr["ones_row"]),
                   (ALIBIK, dr["alibi_k"]), (CLAMPK, dr["clampk"]),
                   (IOTA64[0:64, :], dr["iota64"]),
                   (SGRP[0:64, :], dr["Sgrp"]), (GMASK[0:64, :], dr["Gmask"]),
                   (SG8[0:64, :], dr["Sg8"]),
                   (COS2K, dr["COS2_K"]), (SIN2K, dr["SIN2_K"]),
                   (COS2Q, dr["COS2_Q"]), (SIN2Q, dr["SIN2_Q"])]:
        ld_const = p.dma("sy", s_, d_)
    wl_inv = p.dma("gp", INVF[0:1, 0:64], dr["inv_flat"])

    x_r = dr["x_perm"].rearrange("(tt p) (hc q) -> tt hc p q", p=128, q=1024)
    xld = {}
    NSL = 6
    XL6 = XL + [f32v(73728, 1024), f32v(77824, 1024)]
    ve_xl_free = [0] * NSL
    se_xl_free = [0] * NSL
    te_norm_free = [0, 0]
    (K_S0, K_S1, K_Q0, K_Q1, K_SUM, K_SQ, K_M, K_MSQ, K_VAR, K_TMP,
     K_R, K_NMR) = range(12)

    def sc2(pp, kind):
        b = pp * 24 + kind * 2
        return SCR[:, b:b + 2]

    def sc1(pp, kind, ti):
        b = pp * 24 + kind * 2 + ti
        return SCR[:, b:b + 1]

    ldc = p.cnt["LD"]
    p.wait("ve", "LD", ldc)
    p.wait("se", "LD", ldc)
    p.wait("te", "LD", ldc)
    p.emit("ve", lambda v: v.memset(SCR[:, 62:63], EPS))
    prev_nv_tok = [0, 0]
    evac_done = []
    evac_prev = [0, 0]
    sums_tok = [0] * 8
    sq_tok_p = [0] * 8
    norm_last = [0] * 8
    c6_tok = [0] * 8

    def slot_of(tt, hc):
        return (tt * 2 + hc) % NSL

    def emit_load_half(tt, hc):
        sl = slot_of(tt, hc)
        p.wait("sy", "VE", ve_xl_free[sl])
        p.wait("sy", "SE", se_xl_free[sl])
        xld[(tt, hc)] = (f"LX{sl}", p.dma("sy", XL6[sl], x_r[tt, hc], sem=f"LX{sl}"))

    def emit_sums(k):
        pp = k % 2
        if k >= 2:
            p.wait("se", "VE", norm_last[k - 2])
        tok = 0
        for ti in (0, 1):
            tt = 2 * k + ti
            for hc in (0, 1):
                p.wait("se", *xld[(tt, hc)])
                tok = p.inc("se", lambda s_, sl=slot_of(tt, hc),
                            c=sc1(pp, (K_S0, K_S1)[hc], ti): s_.activation(
                                HJUNK, XL6[sl], AF.Copy, accum_out=c), "SE")
                se_xl_free[slot_of(tt, hc)] = tok
        sums_tok[k] = tok

    def emit_squares(k):
        pp = k % 2
        tok = 0
        for ti in (0, 1):
            tt = 2 * k + ti
            for hc in (0, 1):
                p.wait("ve", *xld[(tt, hc)])
                tok = p.inc("ve", lambda v, sl=slot_of(tt, hc),
                            c=sc1(pp, (K_Q0, K_Q1)[hc], ti): v.scalar_tensor_tensor(
                                out=VJUNK, in0=XL6[sl], scalar=1.0, in1=XL6[sl],
                                op0=ALU.mult, op1=ALU.mult, accum_out=c), "VE")
        sq_tok_p[k] = tok

    def emit_chain(k):
        pp = k % 2
        p.wait("ve", "VE", sq_tok_p[k])
        p.inc("ve", lambda v, pp=pp: v.tensor_tensor(
            out=sc2(pp, K_SQ), in0=sc2(pp, K_Q0), in1=sc2(pp, K_Q1),
            op=ALU.add), "VE")
        p.wait("ve", "SE", sums_tok[k])
        c1 = p.inc("ve", lambda v, pp=pp: v.tensor_tensor(
            out=sc2(pp, K_SUM), in0=sc2(pp, K_S0), in1=sc2(pp, K_S1),
            op=ALU.add), "VE")
        p.wait("ve", "VE", c1)
        c2 = p.inc("ve", lambda v, pp=pp: v.tensor_scalar(
            out=sc2(pp, K_M), in0=sc2(pp, K_SUM), scalar1=1.0 / D, scalar2=None,
            op0=ALU.mult), "VE")
        p.wait("ve", "VE", c2)
        c3 = p.inc("ve", lambda v, pp=pp: v.tensor_tensor(
            out=sc2(pp, K_MSQ), in0=sc2(pp, K_M), in1=sc2(pp, K_M),
            op=ALU.mult), "VE")
        p.wait("ve", "VE", c3)
        c5 = p.inc("ve", lambda v, pp=pp: v.scalar_tensor_tensor(
            out=sc2(pp, K_VAR), in0=sc2(pp, K_SQ), scalar=1.0 / D,
            in1=sc2(pp, K_MSQ), op0=ALU.mult, op1=ALU.subtract), "VE")
        p.wait("se", "VE", c5)
        s1 = p.inc("se", lambda s_, pp=pp: s_.activation(
            sc2(pp, K_TMP), sc2(pp, K_VAR), AF.Ln, bias=SCR[:, 62:63]), "SE")
        p.wait("se", "SE", s1)
        s2 = p.inc("se", lambda s_, pp=pp: s_.activation(
            sc2(pp, K_R), sc2(pp, K_TMP), AF.Exp, scale=-0.5), "SE")
        p.wait("ve", "SE", s2)
        c6_tok[k] = p.inc("ve", lambda v, pp=pp: v.scalar_tensor_tensor(
            out=sc2(pp, K_NMR), in0=sc2(pp, K_M), scalar=-1.0, in1=sc2(pp, K_R),
            op0=ALU.mult, op1=ALU.mult), "VE")

    def emit_norm_tile(k, ti):
        pp = k % 2
        tt = 2 * k + ti
        p.wait("ve", "VE", c6_tok[k])
        for hc in (0, 1):
            sl = slot_of(tt, hc)
            p.wait("ve", "TE", te_norm_free[hc])
            nv_tok = p.inc("ve", lambda v, sl=sl, pp=pp, ti=ti, hc=hc: v.tensor_scalar(
                out=XLNORMP[hc], in0=XL6[sl], scalar1=sc1(pp, K_R, ti),
                scalar2=sc1(pp, K_NMR, ti), op0=ALU.mult, op1=ALU.add), "VE")
            prev_nv_tok[hc] = nv_tok
            ve_xl_free[sl] = nv_tok
        for hc in (0, 1):
            nv_tok = prev_nv_tok[hc]
            p.wait("te", "VE", nv_tok)
            ps = PS[2 + hc]
            p.wait("te", "SE", evac_prev[hc])
            for j in range(8):
                fn = (lambda t_, j=j, ps=ps, hc=hc: nc.tensor.transpose(
                    ps[:, j * 128:(j + 1) * 128],
                    XLNORMP[hc][:, j * 128:(j + 1) * 128], IDNF))
                if j == 7:
                    te_norm_free[hc] = p.inc("te", fn, "TE")
                else:
                    p.emit("te", fn)
            p.wait("se", "TE", te_norm_free[hc])
            ev = p.inc("se", lambda s_, ps=ps, hc=hc, tt=tt: s_.copy(
                out=XLN.rearrange("p (hh q) -> p hh q", q=2048)[:, hc * 8:hc * 8 + 8,
                    tt * 128:(tt + 1) * 128],
                in_=ps.rearrange("p (a b) -> p a b", b=128)), "SE")
            evac_prev[hc] = ev
            evac_done.append(ev)
        norm_last[k] = prev_nv_tok[1]

    for tt0 in (0, 1, 2):
        emit_load_half(tt0, 0)
        emit_load_half(tt0, 1)
    for k in range(8):
        emit_sums(k)
        emit_squares(k)
        emit_chain(k)
        emit_norm_tile(k, 0)
        if 2 * k + 3 <= 15:
            emit_load_half(2 * k + 3, 0)
            emit_load_half(2 * k + 3, 1)
        emit_norm_tile(k, 1)
        if 2 * k + 4 <= 15:
            emit_load_half(2 * k + 4, 0)
            emit_load_half(2 * k + 4, 1)
    p1a_te = p.cnt["TE"]; p1a_ve = p.cnt["VE"]
    taps_avail["XLN"] = (XLN, BF16, "SE", p.cnt["SE"])
    q_rope_done = kv_rope_done = 0
    if sub < 1:
        upto = 0
    if sub >= 1:

        wk_r = dr["Wk"].rearrange("(hh p) (c q) -> c p hh q", p=128, q=128)
        wv_r = dr["Wv"].rearrange("(hh p) (c q) -> c p hh q", p=128, q=128)
        kv_wl = []
        te_wkv_free = [0, 0]
        ps01_free_ve = 0
        kv_rope_done = 0
        raw_free_te = 0
        raw_free_ve = 0
        raw_prev_se = 0
        vttf_prev_ve = p1a_ve
        for c in range(4):
            slot = c % 2
            p.wait("gp", "TE", te_wkv_free[slot])
            wl = p.dma("gp", WKV[slot].rearrange("p (hh q) -> p hh q", q=128), wk_r[c],
                       sem=f"WK{slot}")
            p.wait("te", f"WK{slot}", wl)
            p.wait("te", "VE", ps01_free_ve)
            tok = 0
            for tc in range(4):
                if c == 0:
                    p.wait("te", "SE", evac_done[(tc * 4 + 3) * 2 + 1])
                ps = PS[tc // 2]
                for hh in range(16):
                    fn = (lambda t_, hh=hh, tc=tc, ps=ps, sl=slot: nc.tensor.matmul(
                        ps[:, (tc % 2) * 512:(tc % 2) * 512 + 512],
                        WKV[sl][:, hh * 128:(hh + 1) * 128],
                        XLN[:, hh * 2048 + tc * 512: hh * 2048 + tc * 512 + 512],
                        start=(hh == 0), stop=(hh == 15)))
                    if hh == 15 and tc == 3:
                        tok = p.inc("te", fn, "TE")
                    else:
                        p.emit("te", fn)
            te_wkv_free[slot] = tok
            for ch in range(2):
                psK = PS[ch]
                psW = PS[2 + ch]
                sl = slice(ch * 1024, ch * 1024 + 1024)
                p.wait("se", "TE", tok)
                p.wait("se", "TE", raw_free_te)
                p.wait("se", "VE", raw_free_ve)
                p.wait("se", "SE", raw_prev_se)
                se_tok = p.inc("se", lambda s_, psK=psK: s_.copy(out=RAW, in_=psK[:]), "SE")
                raw_prev_se = se_tok
                p.wait("te", "SE", se_tok)
                jtok = 0
                for half_ in range(2):
                    fn = (lambda t_, psW=psW, half_=half_: nc.tensor.matmul(
                        psW[:, half_ * 512:half_ * 512 + 512], JROT,
                        RAW[:, half_ * 512:half_ * 512 + 512], start=True, stop=True))
                    jtok = p.inc("te", fn, "TE") if half_ == 1 else (p.emit("te", fn) or jtok)
                raw_free_te = jtok
                p.wait("ve", "SE", se_tok)
                p.wait("ve", "VE", vttf_prev_ve)
                raw_free_ve = p.inc("ve", lambda v, sl=sl: v.tensor_tensor(
                    out=VTTF[:, 0:1024], in0=RAW, in1=COS2K[:, sl], op=ALU.mult), "VE")
                p.wait("ve", "TE", jtok)
                sm_tok = p.inc("ve", lambda v, psW=psW, sl=sl: v.tensor_tensor(
                    out=VTTF[:, 1024:2048], in0=psW[:], in1=SIN2K[:, sl], op=ALU.mult), "VE")
                p.wait("ve", "VE", sm_tok)
                ev = p.inc("ve", lambda v, c=c, sl=sl: v.tensor_tensor(
                    out=KT[:, c * 2048 + sl.start: c * 2048 + sl.stop],
                    in0=VTTF[:, 0:1024], in1=VTTF[:, 1024:2048], op=ALU.add), "VE")
                vttf_prev_ve = ev
            ps01_free_ve = ev
            kv_rope_done = ev
        taps_avail["KT"] = (KT, BF16, "VE", p.cnt["VE"])

        for c in range(4 if sub >= 2 else 0):
            slot = c % 2
            p.wait("gp", "TE", te_wkv_free[slot])
            wl = p.dma("gp", WKV[slot].rearrange("p (hh q) -> p hh q", q=128), wv_r[c],
                       sem=f"WK{slot}")
            p.wait("te", f"WK{slot}", wl)
            p.wait("te", "VE", ps01_free_ve)
            tok = 0
            for tc in range(4):
                ps = PS[tc // 2]
                for hh in range(16):
                    fn = (lambda t_, hh=hh, tc=tc, ps=ps, sl=slot: nc.tensor.matmul(
                        ps[:, (tc % 2) * 512:(tc % 2) * 512 + 512],
                        WKV[sl][:, hh * 128:(hh + 1) * 128],
                        XLN[:, hh * 2048 + tc * 512: hh * 2048 + tc * 512 + 512],
                        start=(hh == 0), stop=(hh == 15)))
                    if hh == 15 and tc == 3:
                        tok = p.inc("te", fn, "TE")
                    else:
                        p.emit("te", fn)
            te_wkv_free[slot] = tok
            p.wait("ve", "TE", tok)
            p.wait("ve", "VE", vttf_prev_ve)
            ev0 = p.inc("ve", lambda v: v.tensor_copy(out=VTTF[:, 0:1024], in_=PS[0][:]), "VE")
            ev1 = p.inc("ve", lambda v: v.tensor_copy(out=VTTF[:, 1024:2048], in_=PS[1][:]), "VE")
            ps01_free_ve = ev1
            p.wait("te", "VE", ev1)
            for grp in (0, 1):
                ps = PS[2 + grp]
                for j in range(8):
                    tb = grp * 8 + j
                    fn = (lambda t_, tb=tb, j=j, ps=ps: nc.tensor.transpose(
                        ps[:, j * 128:(j + 1) * 128],
                        VTTF[:, tb * 128:(tb + 1) * 128], IDNF))
                    if j == 7:
                        ttok = p.inc("te", fn, "TE")
                    else:
                        p.emit("te", fn)
                p.wait("ve", "TE", ttok)
                ev = p.inc("ve", lambda v, ps=ps, c=c, grp=grp: v.tensor_copy(
                    out=VTOK.rearrange("p (tt f) -> p tt f", f=512)[:, grp * 8:grp * 8 + 8,
                        c * 128:(c + 1) * 128],
                    in_=ps.rearrange("p (a b) -> p a b", b=128)), "VE")
                p.wait("te", "VE", ev)
        taps_avail["VTOK"] = (VTOK, BF16, "VE", p.cnt["VE"])

        wq_r = dr["Wq"].rearrange("(hh p) (h q) -> h p hh q", p=128, q=128)
        p.wait("ve", "LX2", p.cnt["LX2"])
        p.wait("ve", "LX3", p.cnt["LX3"])
        p.wait("ve", "LX4", p.cnt["LX4"])
        p.wait("ve", "LX5", p.cnt["LX5"])
        te_wq_free = [p1a_te, p1a_te]
        q_evac = [0] * NH
        for h in range(NH if sub >= 3 else 0):
            slot = h % 2
            p.wait("gp", "TE", te_wq_free[slot])
            wl = p.dma("gp", WSL[slot].rearrange("p (hh q) -> p hh q", q=128), wq_r[h],
                       sem=f"WL{slot}")
            p.wait("te", f"WL{slot}", wl)
            ps = PS[h % 2]
            p.wait("te", "VE", ps01_free_ve if h < 2 else q_evac[h - 2])
            tok = 0
            for hh in range(16):
                for tc in range(2):
                    fn = (lambda t_, hh=hh, tc=tc, ps=ps, sl=slot: nc.tensor.matmul(
                        ps[:, tc * 512: tc * 512 + 512],
                        WSL[sl][:, hh * 128:(hh + 1) * 128],
                        XLN[:, hh * 2048 + tc * 512: hh * 2048 + tc * 512 + 512],
                        start=(hh == 0), stop=(hh == 15)))
                    if hh == 15 and tc == 1:
                        tok = p.inc("te", fn, "TE")
                    else:
                        p.emit("te", fn)
            te_wq_free[slot] = tok
            psW = PS[2 + h % 2]
            p.wait("se", "TE", tok)
            p.wait("se", "TE", raw_free_te)
            p.wait("se", "VE", raw_free_ve)
            p.wait("se", "SE", raw_prev_se)
            se_tok = p.inc("se", lambda s_, ps=ps: s_.copy(out=RAW, in_=ps[:]), "SE")
            raw_prev_se = se_tok
            p.wait("te", "SE", se_tok)
            jtok = 0
            for half_ in range(2):
                fn = (lambda t_, psW=psW, half_=half_: nc.tensor.matmul(
                    psW[:, half_ * 512:half_ * 512 + 512], JROT,
                    RAW[:, half_ * 512:half_ * 512 + 512], start=True, stop=True))
                jtok = p.inc("te", fn, "TE") if half_ == 1 else (p.emit("te", fn) or jtok)
            raw_free_te = jtok
            p.wait("ve", "SE", se_tok)
            p.wait("ve", "VE", vttf_prev_ve)
            raw_free_ve = p.inc("ve", lambda v: v.tensor_tensor(
                out=VTTF[:, 0:1024], in0=RAW, in1=COS2Q, op=ALU.mult), "VE")
            p.wait("ve", "TE", jtok)
            sm_tok = p.inc("ve", lambda v, psW=psW: v.tensor_tensor(
                out=VTTF[:, 1024:2048], in0=psW[:], in1=SIN2Q, op=ALU.mult), "VE")
            p.wait("ve", "VE", sm_tok)
            ev = p.inc("ve", lambda v, h=h: v.tensor_tensor(
                out=QT[:, h * 1024:(h + 1) * 1024],
                in0=VTTF[:, 0:1024], in1=VTTF[:, 1024:2048], op=ALU.add), "VE")
            vttf_prev_ve = ev
            q_evac[h] = ev
        taps_avail["QT"] = (QT, BF16, "VE", p.cnt["VE"])
        q_rope_done = q_evac[NH - 1] if sub >= 3 else 0
    p1_te = p.cnt["TE"]; p1_ve = p.cnt["VE"]

    state = dict(nc=nc, p=p, dr=dr, out=out, PS=PS, taps_avail=taps_avail,
                 XLN=XLN, MASKS=MASKS, ALQ=ALQ, X1T=X1T, QT=QT, XG=XG,
                 OT=OT, XLN2=XLN2, KT=KT, VTOK=VTOK, APR=APR, WET=WET, FLT=FLT,
                 SOFTT=SOFTT, XL=XL, NUMT=NUMT, GWT=GWT, SCBC=SCBC, RING=RING,
                 RAW=RAW, NEGMR=NEGMR, ATMP=ATMP, WSL=WSL, GGRP=GGRP, MUTA=MUTA,
                 IDNBF=IDNBF, IDNF=IDNF, ONESBF=ONESBF, ONESF=ONESF,
                 ONESROW=ONESROW, ALIBIK=ALIBIK, CLAMPK=CLAMPK, IOTA64=IOTA64, SGRP=SGRP,
                 PERM=PERM, INVF=INVF, GMASK=GMASK, SOFTG=SOFTG, RCOLS=RCOLS,
                 WT2=NUMT, SG8=SG8,
                 SCR=SCR, bfv=bfv, f32v=f32v, wl_inv=wl_inv,
                 p1_tokens=dict(q_rope_done=q_rope_done, kv_rope_done=kv_rope_done,
                                p1_te=p1_te, p1_ve=p1_ve))

    if upto >= 2:
        build_p2(state)
    if upto >= 3:
        build_p3(state)
    if upto >= 4:
        build_p4(state)

    tap_out = {}
    for name in taps:
        buf, dt_, sem, val = state["taps_avail"][name]
        tshape = [128, buf.shape[1]]
        tap_out[name] = nc.dram_tensor("tap_" + name, tshape, dt_, kind="ExternalOutput").ap()
        p.wait("sy", sem, val)
        p.outdma(tap_out[name], buf)

    if upto < 4:
        p.wait("ve", "OUT", p.cnt["OUT"])
        tok = p.inc("ve", lambda v: v.memset(f32v(0, 2048), 0.0), "VE")
        p.wait("sy", "VE", tok)
        o_r = out.rearrange("(tt p) d -> tt p d", p=128)
        for tt in range(8):
            p.outdma(o_r[tt], f32v(0, 2048))
    for s in ("LD", "WL", "LX0", "LX1", "LX2", "LX3", "LX4", "LX5",
              "WL0", "WL1", "WK0", "WK1", "OU0", "OU1", "OU2", "OU3"):
        p.wait("sy", s, p.cnt[s])
    p.wait("sy", "OUT", p.cnt["OUT"])

    with nc.Block() as blk:
        @blk.sync
        def _(sy):
            for fn in p.ops["sy"]:
                fn(sy)
        @blk.gpsimd
        def _(gp):
            for fn in p.ops["gp"]:
                fn(gp)
        @blk.tensor
        def _(te):
            for fn in p.ops["te"]:
                fn(te)
        @blk.vector
        def _(ve):
            for fn in p.ops["ve"]:
                fn(ve)
        @blk.scalar
        def _(se):
            for fn in p.ops["se"]:
                fn(se)
    return nc, [k for k in tap_out]


def build_p2(st):
    p = st["p"]; nc = st["nc"]; dr = st["dr"]; PS = st["PS"]
    QT = st["QT"]; KT = st["KT"]; VTOK = st["VTOK"]; OT = st["OT"]
    MASKS = st["MASKS"]; ALQ = st["ALQ"]
    ONESBF = st["ONESBF"]; ALIBIK = st["ALIBIK"]
    bfv = st["bfv"]; f32v = st["f32v"]
    tk = st["p1_tokens"]

    PSB = [bfv(172032 + k * 1024, 512) for k in range(6)]
    RECIPB = f32v(188416, 512)

    p.wait("sy", "TE", tk["p1_te"])
    masks_ld = p.dma("sy", MASKS, dr["masks"])
    lx2_final = p.cnt["LX2"]; lx3_final = p.cnt["LX3"]
    lx4_final = p.cnt["LX4"]; lx5_final = p.cnt["LX5"]
    aq_sem = ["AQ0", "AQ1"]
    for s in aq_sem:
        if s not in p.sem:
            p.sem[s] = nc.alloc_semaphore(s); p.cnt[s] = 0
    alq_ld = [0, 0]
    alq_ld[0] = p.dma("sy", ALQ[0][0:1, :], dr["alibiq"][0:1, :], sem="AQ0")
    alq_ld[1] = p.dma("sy", ALQ[1][0:1, :], dr["alibiq"][1:2, :], sem="AQ1")

    SCB = [PS[0][:, 0:512], PS[0][:, 512:1024], PS[2][:, 0:512], PS[2][:, 512:1024]]
    slot_exp = [0, 0, 0, 0]
    psb_av = [0] * 6
    first_wait_done = False
    prev_recip_ve = 0
    slot_psb_exp = [0] * 6
    unit_evac = {}
    unit = 0
    for h in range(NH):
        kv = h // 4
        aq = h % 2
        for gq in (0, 1):
            u = unit; unit += 1
            OS = PS[1] if u % 2 == 0 else PS[3]
            sched = _active_tiles(h, gq)
            ntile = len(sched)
            exp_t = [0] * ntile
            mask_t = [0] * ntile
            sc_mm_t = [0] * ntile
            clamp_t = [0] * ntile
            avsum_t = [0] * ntile
            for step in range(ntile + 3):
                i = step
                if i < ntile:
                    kind, kp, s, slot = sched[i]
                    ncols = 512 - 128 * s
                    qoff = h * 1024 + gq * 512 + 128 * s
                    kcol = kv * 2048 + kp * 128
                    sl4 = (u * 24 + i) % 4
                    sl6 = (u * 24 + i) % 6
                    scs = SCB[sl4]
                    need_r1 = _SLOPES[h] * 2047.0 > 70.0
                    p.wait("te", "SE", slot_exp[sl4])
                    if i == 0:
                        if u >= 2:
                            p.wait("te", "VE", unit_evac[u - 2])
                        if need_r1:
                            p.wait("te", aq_sem[aq], alq_ld[aq])
                        if not first_wait_done:
                            p.wait("te", "VE", tk["q_rope_done"])
                            first_wait_done = True
                    if need_r1:
                        p.emit("te", lambda t_, scs=scs, kcol=kcol, qoff=qoff, ncols=ncols:
                               nc.tensor.matmul(scs[:, 0:ncols], KT[:, kcol:kcol + 128],
                                                QT[:, qoff:qoff + ncols],
                                                start=True, stop=False))
                        sc_mm_t[i] = p.inc("te", lambda t_, scs=scs, aq=aq, gq=gq, s=s,
                               ncols=ncols: nc.tensor.matmul(
                                   scs[:, 0:ncols], ONESBF[0:1, 0:128],
                                   ALQ[aq][0:1, gq * 512 + 128 * s:
                                           gq * 512 + 128 * s + ncols],
                                   start=False, stop=True), "TE")
                    else:
                        sc_mm_t[i] = p.inc("te", lambda t_, scs=scs, kcol=kcol,
                               qoff=qoff, ncols=ncols: nc.tensor.matmul(
                                   scs[:, 0:ncols], KT[:, kcol:kcol + 128],
                                   QT[:, qoff:qoff + ncols],
                                   start=True, stop=True), "TE")
                    if h <= 3:
                        p.wait("ve", "TE", sc_mm_t[i])
                        clamp_t[i] = p.inc("ve", lambda v, scs=scs, h=h, kp=kp:
                            v.tensor_scalar_min(
                                scs[:, 0:128], scs[:, 0:128],
                                st["CLAMPK"][:, h * 16 + kp: h * 16 + kp + 1]), "VE")
                    p.wait("se", "TE", sc_mm_t[i])
                    if clamp_t[i]:
                        p.wait("se", "VE", clamp_t[i])
                    p.wait("se", "TE", psb_av[sl6])
                    p.wait("se", "SE", slot_psb_exp[sl6])
                    if u == 0 and i < 6:
                        p.wait("se", "LD", masks_ld)
                        p.wait("se", "LX2", lx2_final)
                        p.wait("se", "LX3", lx3_final)
                        p.wait("se", "LX4", lx4_final)
                        p.wait("se", "LX5", lx5_final)
                        p.wait("se", "VE", tk["q_rope_done"])
                        p.wait("se", "WK0", p.cnt["WK0"])
                        p.wait("se", "WK1", p.cnt["WK1"])
                    exp_t[i] = p.inc("se", lambda s_, scs=scs, h=h, kp=kp,
                           ncols=ncols, sl6=sl6: s_.activation(
                               PSB[sl6][:, 0:ncols], scs[:, 0:ncols], AF.Exp,
                               bias=ALIBIK[:, h * 16 + kp: h * 16 + kp + 1],
                               scale=C_SCALE), "SE")
                    slot_exp[sl4] = exp_t[i]
                    slot_psb_exp[sl6] = exp_t[i]
                    p.wait("ve", "SE", exp_t[i])
                    if u == 0 and i < 4:
                        p.wait("ve", "LD", masks_ld)
                    mask_t[i] = p.inc("ve", lambda v, sl6=sl6, slot=slot: v.tensor_tensor(
                        out=PSB[sl6][:, 0:128], in0=PSB[sl6][:, 0:128],
                        in1=MASKS[:, slot * 128:(slot + 1) * 128], op=ALU.mult), "VE")
                j = step - 3
                if 0 <= j < ntile:
                    kindj, kpj, sj, slotj = sched[j]
                    ncolsj = 512 - 128 * sj
                    sl4j = (u * 24 + j) % 6
                    p.wait("te", "VE", mask_t[j])
                    first = (j == 0)
                    last = (j == ntile - 1)
                    p.emit("te", lambda t_, OS=OS, sj=sj, kpj=kpj, kv=kv, sl4j=sl4j,
                           ncolsj=ncolsj, first=first, last=last: nc.tensor.matmul(
                               OS[:, 128 * sj:512],
                               VTOK[:, (kpj * 4 + kv) * 128:(kpj * 4 + kv + 1) * 128],
                               PSB[sl4j][:, 0:ncolsj], start=first, stop=last))
                    avsum_t[j] = p.inc("te", lambda t_, OS=OS, sj=sj, sl4j=sl4j,
                           ncolsj=ncolsj, first=first, last=last: nc.tensor.matmul(
                               OS[:, 512 + 128 * sj:1024], ONESBF,
                               PSB[sl4j][:, 0:ncolsj], start=first, stop=last), "TE")
                    psb_av[sl4j] = avsum_t[j]
            p.wait("ve", "TE", avsum_t[ntile - 1])
            p.wait("ve", "VE", prev_recip_ve)
            r1 = p.inc("ve", lambda v, OS=OS: v.reciprocal(
                out=RECIPB, in_=OS[:, 512:1024]), "VE")
            prev_recip_ve = r1
            p.wait("ve", "VE", r1)
            ev = p.inc("ve", lambda v, OS=OS, h=h, gq=gq: v.tensor_tensor(
                out=OT[:, h * 1024 + gq * 512: h * 1024 + gq * 512 + 512],
                in0=OS[:, 0:512], in1=RECIPB, op=ALU.mult), "VE")
            unit_evac[u] = ev
        if h + 1 < NH and _SLOPES[h + 1] * 2047.0 > 70.0:
            nxt = (h + 1) % 2
            if h >= 1:
                p.wait("sy", "VE", unit_evac[(h - 1) * 2 + 1])
            alq_ld[nxt] = p.dma("sy", ALQ[nxt][0:1, :],
                                dr["alibiq"][h + 1:h + 2, :], sem=aq_sem[nxt])
    st["taps_avail"]["OT"] = (OT, BF16, "VE", p.cnt["VE"])
    st["p2_tokens"] = dict(last_evac=unit_evac[unit - 1], p2_te=p.cnt["TE"],
                           p2_ve=p.cnt["VE"])


def build_p3(st):
    p = st["p"]; nc = st["nc"]; dr = st["dr"]; PS = st["PS"]
    OT = st["OT"]; X1T = st["X1T"]; XLN2 = st["XLN2"]; XG = st["XG"]
    XL = st["XL"]; WSL = st["WSL"]; GWT = st["GWT"]
    ONESF = st["ONESF"]; ONESBF = st["ONESBF"]; ONESROW = st["ONESROW"]
    IDNF = st["IDNF"]; IOTA64 = st["IOTA64"]; SGRP = st["SGRP"]
    PERM = st["PERM"]; INVF = st["INVF"]; NEGMR = st["NEGMR"]; RCOLS = st["RCOLS"]
    WET = st["WET"]; FLT = st["FLT"]; SOFTT = st["SOFTT"]; NUMT = st["NUMT"]
    WT2 = st["WT2"]; SG8 = st["SG8"]; SCR = st["SCR"]
    assert WT2 is NUMT or True
    f32v = st["f32v"]
    tk1 = st["p1_tokens"]; tk2 = st["p2_tokens"]
    x_r = dr["x_perm"].rearrange("(tt p) (hc q) -> tt hc p q", p=128, q=1024)
    wo_r = dr["Wo"].rearrange("(dh p) (h2 q) -> h2 p dh q", p=128, q=128)

    MS = f32v(131072, 1024)
    RSTD = f32v(135168, 1024)
    VTMP = f32v(139264, 1024)
    ES = f32v(143360, 512)
    RZ = f32v(145408, 1024)
    attn_ve = tk2["p2_ve"]; attn_te = tk2["p2_te"]

    xld = []
    lx_free = [("VE", 0), ("VE", 0)]
    tp_tok = {}
    copy_tok = {}
    for tt in range(OWN):
        for hc in (0, 1):
            slot = hc
            eng, val = lx_free[slot]
            p.wait("sy", eng, val)
            xld.append((f"LX{slot}", p.dma("sy", XL[slot], x_r[tt, hc], sem=f"LX{slot}")))
            ps = PS[2 + hc]
            p.wait("te", *xld[-1])
            if tt == 0:
                p.wait("te", "VE", tk2["last_evac"])
            if tt >= 1:
                p.wait("te", "SE", copy_tok[(tt - 1, hc)])
            for j in range(8):
                fn = (lambda t_, j=j, ps=ps, sl=slot: nc.tensor.transpose(
                    ps[:, j * 128:(j + 1) * 128], XL[sl][:, j * 128:(j + 1) * 128], IDNF))
                tok = p.inc("te", fn, "TE") if j == 7 else (p.emit("te", fn) or 0)
            tp_tok[(tt, hc)] = tok
            p.wait("se", "TE", tok)
            if tt == 0 and hc == 0:
                p.wait("se", "VE", attn_ve)
                p.wait("se", "TE", attn_te)
            ct = p.inc("se", lambda s_, ps=ps, hc=hc, tt=tt: s_.copy(
                out=X1T.rearrange("p (hh q) -> p hh q", q=1024)[:, hc * 8:hc * 8 + 8,
                    tt * 128:(tt + 1) * 128],
                in_=ps.rearrange("p (a b) -> p a b", b=128)), "SE")
            copy_tok[(tt, hc)] = ct
            lx_free[slot] = ("SE", ct)
    xT_done = copy_tok[(OWN - 1, 1)]

    wo_free = [tk1["p1_te"], tk1["p1_te"]]
    add_tok = [0] * 16
    for h2 in range(16):
        slot = h2 % 2
        p.wait("gp", "TE", wo_free[slot])
        wl = p.dma("gp", WSL[slot].rearrange("p (dh q) -> p dh q", q=128), wo_r[h2],
                   sem=f"WL{slot}")
        ps = PS[h2 % 2]
        p.wait("te", f"WL{slot}", wl)
        if h2 == 0:
            p.wait("te", "VE", tk2["last_evac"])
        if h2 >= 2:
            p.wait("te", "VE", add_tok[h2 - 2])
        tok = 0
        for dh in range(16):
            for tc in range(2):
                fn = (lambda t_, dh=dh, tc=tc, ps=ps, sl=slot: nc.tensor.matmul(
                    ps[:, tc * 512:tc * 512 + 512],
                    WSL[sl][:, dh * 128:(dh + 1) * 128],
                    OT[:, dh * 1024 + tc * 512: dh * 1024 + tc * 512 + 512],
                    start=(dh == 0), stop=(dh == 15)))
                tok = p.inc("te", fn, "TE") if (dh == 15 and tc == 1) else (p.emit("te", fn) or tok)
        wo_free[slot] = tok
        p.wait("ve", "TE", tok)
        p.wait("ve", "SE", xT_done)
        add_tok[h2] = p.inc("ve", lambda v, ps=ps, h2=h2: v.tensor_tensor(
            out=X1T[:, h2 * 1024:(h2 + 1) * 1024],
            in0=X1T[:, h2 * 1024:(h2 + 1) * 1024], in1=ps[:], op=ALU.add), "VE")
    x1_done = add_tok[15]
    st["taps_avail"]["X1T"] = (X1T, F32, "VE", p.cnt["VE"])

    sq_tok = [0] * 16
    for hh in range(16):
        p.wait("se", "VE", add_tok[hh])
        sq_tok[hh] = p.inc("se", lambda s_, hh=hh: s_.activation(
            XG[:, hh * 1024:(hh + 1) * 1024], X1T[:, hh * 1024:(hh + 1) * 1024],
            AF.Square), "SE")
    tokm = tokq = 0
    for hh in range(16):
        p.wait("te", "VE", add_tok[hh])
        p.wait("te", "SE", sq_tok[hh])
        for tc in range(2):
            fnm = (lambda t_, hh=hh, tc=tc: nc.tensor.matmul(
                PS[2][:, tc * 512:tc * 512 + 512], ONESF,
                X1T[:, hh * 1024 + tc * 512: hh * 1024 + tc * 512 + 512],
                start=(hh == 0), stop=(hh == 15)))
            fnq = (lambda t_, hh=hh, tc=tc: nc.tensor.matmul(
                PS[3][:, tc * 512:tc * 512 + 512], ONESBF,
                XG[:, hh * 1024 + tc * 512: hh * 1024 + tc * 512 + 512],
                start=(hh == 0), stop=(hh == 15)))
            tokm = p.inc("te", fnm, "TE")
            tokq = p.inc("te", fnq, "TE")
    p.wait("ve", "TE", tokq)
    t1 = p.inc("ve", lambda v: v.tensor_scalar(out=MS, in0=PS[2][:], scalar1=1.0 / D,
               scalar2=None, op0=ALU.mult), "VE")
    p.wait("ve", "VE", t1)
    t2 = p.inc("ve", lambda v: v.tensor_tensor(out=VTMP, in0=MS, in1=MS, op=ALU.mult), "VE")
    p.wait("ve", "VE", t2)
    t3 = p.inc("ve", lambda v: v.scalar_tensor_tensor(
        out=VTMP, in0=PS[3][:], scalar=1.0 / D, in1=VTMP,
        op0=ALU.mult, op1=ALU.subtract), "VE")
    p.wait("se", "VE", t3)
    s1 = p.inc("se", lambda s_: s_.activation(RSTD, VTMP, AF.Ln, bias=SCR[:, 62:63]), "SE")
    p.wait("se", "SE", s1)
    s2 = p.inc("se", lambda s_: s_.activation(RSTD, RSTD, AF.Exp, scale=-0.5), "SE")
    p.wait("se", "SE", s2)
    s3 = p.inc("se", lambda s_: s_.mul(out=NEGMR[0:1, :], in_=MS[0:1, :], mul=-1.0), "SE")
    xln2_tok = [0] * 16
    prev_x2v = 0
    for hh in range(16):
        p.wait("ve", "SE", s2)
        p.wait("ve", "VE", t3)
        p.wait("ve", "VE", prev_x2v)
        a1 = p.inc("ve", lambda v, hh=hh: v.tensor_tensor(
            out=VTMP, in0=X1T[:, hh * 1024:(hh + 1) * 1024], in1=MS, op=ALU.subtract), "VE")
        p.wait("ve", "VE", a1)
        xln2_tok[hh] = p.inc("ve", lambda v, hh=hh: v.tensor_tensor(
            out=XLN2[:, hh * 1024:(hh + 1) * 1024], in0=VTMP, in1=RSTD, op=ALU.mult), "VE")
        prev_x2v = xln2_tok[hh]
    st["taps_avail"]["XLN2"] = (XLN2, BF16, "VE", p.cnt["VE"])

    p.wait("sy", "TE", tp_tok[(OWN - 1, 1)])
    gwt_ld = p.dma("sy", GWT, dr["gate_wT"].rearrange("(hh p) e -> p hh e", p=128))
    p.wait("te", "LD", gwt_ld)
    p.wait("te", "VE", t3)
    ctok = 0
    for hh in range(16):
        fn = (lambda t_, hh=hh: nc.tensor.matmul(
            PS[1][:, 0:64], ONESF, GWT[:, hh * 64:(hh + 1) * 64],
            start=(hh == 0), stop=(hh == 15)))
        ctok = p.inc("te", fn, "TE") if hh == 15 else (p.emit("te", fn) or ctok)
    p.wait("se", "TE", ctok)
    cs_tok = p.inc("se", lambda s_: s_.copy(out=SCR[0:1, 0:64], in_=PS[1][0:1, 0:64]), "SE")
    rtok = 0
    p.wait("te", "SE", s2)
    for tt in range(8):
        fn = (lambda t_, tt=tt: nc.tensor.matmul(
            PS[1][:, 512 + tt:512 + tt + 1], RSTD[0:1, tt * 128:(tt + 1) * 128],
            ONESROW[0:1, 0:1], start=True, stop=True))
        rtok = p.inc("te", fn, "TE")
    p.wait("se", "TE", rtok)
    rc_tok = p.inc("se", lambda s_: s_.copy(out=RCOLS, in_=PS[1][:, 512:520]), "SE")
    lg_tok = 0
    p.wait("te", "SE", s3)
    p.wait("te", "VE", add_tok[15])
    for tt in range(8):
        for hh in range(16):
            fn = (lambda t_, tt=tt, hh=hh: nc.tensor.matmul(
                PS[0][:, tt * 64:(tt + 1) * 64],
                X1T[:, hh * 1024 + tt * 128: hh * 1024 + tt * 128 + 128],
                GWT[:, hh * 64:(hh + 1) * 64], start=(hh == 0), stop=False))
            p.emit("te", fn)
        p.wait("te", "SE", cs_tok)
        lg_tok = p.inc("te", lambda t_, tt=tt: nc.tensor.matmul(
            PS[0][:, tt * 64:(tt + 1) * 64], NEGMR[0:1, tt * 128:(tt + 1) * 128],
            SCR[0:1, 0:64], start=False, stop=True), "TE")
    LG = f32v(149504, 512)
    p.wait("ve", "TE", lg_tok)
    p.wait("ve", "SE", rc_tok)
    lgs_tok = 0
    for tt in range(8):
        lgs_tok = p.inc("ve", lambda v, tt=tt: v.tensor_scalar(
            out=LG[:, tt * 64:(tt + 1) * 64], in0=PS[0][:, tt * 64:(tt + 1) * 64],
            scalar1=RCOLS[:, tt:tt + 1], scalar2=None, op0=ALU.mult), "VE")
    WE = f32v(151552, 512)
    we_tok = [0] * 8
    for tt in range(8):
        p.wait("se", "VE", lgs_tok)
        e1 = p.inc("se", lambda s_, tt=tt: s_.activation(
            ES[:, tt * 64:(tt + 1) * 64], LG[:, tt * 64:(tt + 1) * 64], AF.Exp,
            accum_out=SCR[:, 16 + tt:17 + tt]), "SE")
        p.wait("ve", "SE", e1)
        m1 = p.inc("ve", lambda v, tt=tt: v.reciprocal(
            out=SCR[:, 24 + tt:25 + tt], in_=SCR[:, 16 + tt:17 + tt]), "VE")
        p.wait("ve", "VE", m1)
        m2 = p.inc("ve", lambda v, tt=tt: v.max(
            out=SCR[:, 32:40], in_=ES[:, tt * 64:(tt + 1) * 64]), "VE")
        p.wait("ve", "VE", m2)
        p.wait("ve", "VE", prev_x2v)
        m3 = p.inc("ve", lambda v, tt=tt: v.tensor_scalar(
            out=VTMP[:, 0:64], in0=ES[:, tt * 64:(tt + 1) * 64],
            scalar1=SCR[:, 39:40], scalar2=None, op0=ALU.is_ge), "VE")
        p.wait("ve", "VE", m3)
        m4 = p.inc("ve", lambda v, tt=tt: v.tensor_tensor(
            out=VTMP[:, 0:64], in0=VTMP[:, 0:64],
            in1=ES[:, tt * 64:(tt + 1) * 64], op=ALU.mult), "VE")
        p.wait("ve", "VE", m4)
        we_tok[tt] = p.inc("ve", lambda v, tt=tt: v.tensor_scalar(
            out=WE[:, tt * 64:(tt + 1) * 64], in0=VTMP[:, 0:64],
            scalar1=SCR[:, 24 + tt:25 + tt], scalar2=None, op0=ALU.mult), "VE")
        prev_x2v = we_tok[tt]
    wt_tok = 0
    for tt in range(8):
        p.wait("te", "VE", we_tok[tt])
        fn = (lambda t_, tt=tt: nc.tensor.transpose(
            PS[2][0:64, tt * 128:(tt + 1) * 128].  __getitem__((slice(None), slice(0, 128))) if False else
            PS[2][0:64, tt * 128:(tt + 1) * 128],
            WE[:, tt * 64:(tt + 1) * 64], IDNF))
        wt_tok = p.inc("te", fn, "TE")
    p.wait("se", "TE", wt_tok)
    wet_tok = p.inc("se", lambda s_: s_.copy(out=WET[0:64, :], in_=PS[2][0:64, 0:1024]), "SE")
    p.wait("te", "WL", st["wl_inv"])
    ib_tok = p.inc("te", lambda t_: nc.tensor.matmul(
        PS[3][0:64, 0:64], ONESROW[0:1, 0:64], INVF[0:1, 0:64],
        start=True, stop=True), "TE")
    p.wait("ve", "TE", ib_tok)
    pm_tok = p.inc("ve", lambda v: v.tensor_scalar(
        out=PERM[0:64, :], in0=PS[3][0:64, 0:64], scalar1=IOTA64[0:64, :],
        scalar2=None, op0=ALU.is_equal), "VE")
    p.wait("te", "VE", pm_tok)
    p.wait("te", "SE", wet_tok)
    ft_tok = 0
    for tc in range(2):
        fn = (lambda t_, tc=tc: nc.tensor.matmul(
            PS[3][0:64, tc * 512:(tc + 1) * 512], PERM[0:64, 0:64],
            WET[0:64, tc * 512:(tc + 1) * 512], start=True, stop=True))
        ft_tok = p.inc("te", fn, "TE")
    p.wait("se", "TE", ft_tok)
    flt_tok = p.inc("se", lambda s_: s_.copy(out=FLT[0:64, :], in_=PS[3][0:64, 0:1024]), "SE")
    p.wait("ve", "SE", flt_tok)
    mt_tok = p.inc("ve", lambda v: v.tensor_scalar(
        out=WET[0:64, :], in0=FLT[0:64, :], scalar1=0.0, scalar2=None,
        op0=ALU.is_gt), "VE")
    p.wait("se", "SE", flt_tok)
    p.wait("se", "VE", prev_x2v)
    ez_tok = p.inc("se", lambda s_: s_.activation(VTMP[0:64, :], FLT[0:64, :], AF.Exp), "SE")
    p.wait("ve", "SE", ez_tok)
    p.wait("ve", "VE", mt_tok)
    nm_tok = p.inc("ve", lambda v: v.tensor_tensor(
        out=NUMT[0:64, :], in0=VTMP[0:64, :], in1=WET[0:64, :], op=ALU.mult), "VE")
    p.wait("te", "VE", nm_tok)
    dn_tok = 0
    for tc in range(2):
        fn = (lambda t_, tc=tc: nc.tensor.matmul(
            PS[2][0:64, tc * 512:(tc + 1) * 512], SGRP[0:64, :],
            NUMT[0:64, tc * 512:(tc + 1) * 512], start=True, stop=True))
        dn_tok = p.inc("te", fn, "TE")
    p.wait("ve", "TE", dn_tok)
    p.wait("ve", "VE", nm_tok)
    d1 = p.inc("ve", lambda v: v.tensor_scalar(
        out=VTMP[0:64, :], in0=PS[2][0:64, 0:1024], scalar1=1e-30, scalar2=None,
        op0=ALU.add), "VE")
    p.wait("ve", "VE", d1)
    d2 = p.inc("ve", lambda v: v.reciprocal(out=RZ[0:64, :], in_=VTMP[0:64, :]), "VE")
    p.wait("ve", "VE", d2)
    p.wait("ve", "VE", nm_tok)
    soft_tok = p.inc("ve", lambda v: v.tensor_tensor(
        out=SOFTT[0:64, :], in0=NUMT[0:64, :], in1=RZ[0:64, :], op=ALU.mult), "VE")
    p.wait("ve", "VE", soft_tok)
    p.wait("ve", "VE", mt_tok)
    wt2_tok = p.inc("ve", lambda v: v.tensor_tensor(
        out=WT2[0:64, :], in0=FLT[0:64, :], in1=WET[0:64, :], op=ALU.mult), "VE")
    st["taps_avail"]["SOFTT"] = (SOFTT, BF16, "VE", soft_tok)
    st["taps_avail"]["WT2"] = (WT2, F32, "VE", wt2_tok)
    st["p3_tokens"] = dict(x1_done=x1_done, xln2_tok=xln2_tok, soft_tok=soft_tok,
                           wt2_tok=wt2_tok, p3_te=p.cnt["TE"], p3_ve=p.cnt["VE"],
                           p3_se=p.cnt["SE"])


def build_p4(st):
    p = st["p"]; nc = st["nc"]; dr = st["dr"]; PS = st["PS"]
    X1T = st["X1T"]; XLN2 = st["XLN2"]; XG = st["XG"]; APR = st["APR"]
    SOFTT = st["SOFTT"]; SOFTG = st["SOFTG"]; GMASK = st["SG8"]
    WT2 = st["WT2"]; GGRP = st["GGRP"]; MUTA = st["MUTA"]; SCBC = st["SCBC"]
    IDNF = st["IDNF"]; bfv = st["bfv"]; f32v = st["f32v"]
    out = st["out"]
    tk3 = st["p3_tokens"]

    SIL = bfv(188416, 1024)
    GSL = [bfv(172032, 2048), bfv(176128, 2048)]
    USL = [bfv(180224, 2048), bfv(184320, 2048)]
    DSL = [bfv(153600, 11 * 128), bfv(157696, 11 * 128)]
    OUTB = [f32v(98304, 2048), f32v(106496, 2048),
            f32v(114688, 2048), f32v(122880, 2048)]

    for s in ("WG0", "WG1", "WU0", "WU1", "WD0", "WD1"):
        if s not in p.sem:
            p.sem[s] = nc.alloc_semaphore(s); p.cnt[s] = 0

    gateT_r = dr["gateT"].rearrange("g (hh p) (it q) -> g it p hh q", p=128, q=128)
    upT_r = dr["upT"].rearrange("g (hh p) (it q) -> g it p hh q", p=128, q=128)
    downT_r = dr["downT"].rearrange("g (it p) (hh q) -> g hh p it q", p=128, q=128)

    p.wait("sy", "TE", tk3["p3_te"])
    ggrp_ld = p.dma("sy", GGRP[0:64, :], dr["Ggrp"])
    p.wait("gp", "TE", tk3["p3_te"])
    muta_ld = p.dma("gp", MUTA[0:64, :], dr["mutaT"])

    gate_free = [tk3["p3_te"], tk3["p3_te"]]
    up_free = [tk3["p3_te"], tk3["p3_te"]]
    down_free = [tk3["p3_te"], tk3["p3_te"]]
    gate_ld = {}; up_ld = {}; down_ld = {}

    def fetch_gate(g, i):
        slot = (g * NI + i) % 2
        p.wait("gp", "TE", gate_free[slot])
        gate_ld[(g, i)] = (f"WG{slot}", p.dma(
            "gp", GSL[slot].rearrange("p (hh q) -> p hh q", q=128),
            gateT_r[g, i], sem=f"WG{slot}"), slot)

    def fetch_up(g, i):
        slot = (g * NI + i) % 2
        p.wait("gp", "TE", up_free[slot])
        up_ld[(g, i)] = (f"WU{slot}", p.dma(
            "gp", USL[slot].rearrange("p (hh q) -> p hh q", q=128),
            upT_r[g, i], sem=f"WU{slot}"), slot)

    def fetch_down(g, hh):
        slot = (g * 16 + hh) % 2
        p.wait("gp", "TE", down_free[slot])
        down_ld[(g, hh)] = (f"WD{slot}", p.dma(
            "gp", DSL[slot].rearrange("p (it q) -> p it q", q=128),
            downT_r[g, hh], sem=f"WD{slot}"), slot)

    p.wait("gp", "VE", tk3["wt2_tok"])
    p.wait("gp", "SE", tk3["p3_se"])
    fetch_gate(0, 0); fetch_up(0, 0)
    fetch_gate(0, 1); fetch_up(0, 1)
    fetch_down(0, 0); fetch_down(0, 1)

    xg_add = [0] * 16
    x2_add = [0] * 16
    apr_tok = [0] * NI
    scbc_ve = 0
    softg_ve = 0
    state = dict(scbc_ve=0, softg_ve=0)

    def emit_scbc_softg(g):
        p.wait("te", "VE", tk3["wt2_tok"] if g == 0 else apr_tok[NI - 1])
        if g == 0:
            p.wait("te", "LD", ggrp_ld)
        sc_tok = 0
        for tc in range(2):
            fn = (lambda t_, g=g, tc=tc: nc.tensor.matmul(
                PS[0][:, tc * 512:(tc + 1) * 512], GGRP[0:64, g * 128:(g + 1) * 128],
                WT2[0:64, tc * 512:(tc + 1) * 512], start=True, stop=True))
            sc_tok = p.inc("te", fn, "TE")
        p.wait("ve", "TE", sc_tok)
        p.wait("ve", "VE", state["scbc_ve"])
        state["scbc_ve"] = p.inc("ve", lambda v: v.tensor_copy(out=SCBC, in_=PS[0][:]), "VE")
        p.wait("ve", "VE", tk3["soft_tok"] if g == 0 else state["softg_ve"])
        state["softg_ve"] = p.inc("ve", lambda v, g=g: v.tensor_scalar(
            out=SOFTG[0:64, :], in0=SOFTT[0:64, :], scalar1=GMASK[0:64, g:g + 1],
            scalar2=None, op0=ALU.mult), "VE")

    def emit_xg_hh(g, hh):
        psX = PS[0] if hh % 2 == 0 else PS[3]
        if hh >= 2:
            p.wait("te", "VE", xg_add[hh - 2])
        elif hh == 0:
            p.wait("te", "VE", state["scbc_ve"])
            p.wait("te", "VE", state["softg_ve"])
            if g == 0:
                p.wait("te", "WL", muta_ld)
        tok = 0
        for tc in range(2):
            fn = (lambda t_, hh=hh, tc=tc, psX=psX: nc.tensor.matmul(
                psX[:, tc * 512:(tc + 1) * 512], MUTA[0:64, hh * 128:(hh + 1) * 128],
                SOFTG[0:64, tc * 512:(tc + 1) * 512], start=True, stop=True))
            tok = p.inc("te", fn, "TE")
        p.wait("ve", "TE", tok)
        p.wait("ve", "VE", xg_add[hh])
        p.wait("ve", "VE", tk3["xln2_tok"][hh])
        xg_add[hh] = p.inc("ve", lambda v, hh=hh, psX=psX: v.tensor_tensor(
            out=XG[:, hh * 1024:(hh + 1) * 1024], in0=psX[:],
            in1=XLN2[:, hh * 1024:(hh + 1) * 1024], op=ALU.add), "VE")

    prev_sil_tok = 0
    prev_sil_se = 0
    for g in range(G):
        if g == 0:
            emit_scbc_softg(0)
            for hh in range(16):
                emit_xg_hh(0, hh)
        for i in range(NI):
            kg, vg, sg = gate_ld[(g, i)]
            ku, vu, su = up_ld[(g, i)]
            psG = PS[0] if i % 2 == 0 else PS[2]
            psU = PS[1] if i % 2 == 0 else PS[3]
            p.wait("te", kg, vg)
            p.wait("te", ku, vu)
            if i >= 2:
                p.wait("te", "VE", apr_tok[i - 2])
            elif i == 0:
                p.wait("te", "VE", xg_add[14])
                if g > 0:
                    p.wait("te", "VE", x2_add[14])
            elif i == 1:
                if g > 0:
                    p.wait("te", "VE", x2_add[15])
                p.wait("te", "VE", xg_add[15])
            tok = 0
            for hh in range(16):
                if hh == 0:
                    p.wait("te", "VE", xg_add[15])
                for tc in range(2):
                    fn = (lambda t_, hh=hh, tc=tc, psG=psG, sl=sg: nc.tensor.matmul(
                        psG[:, tc * 512:(tc + 1) * 512], GSL[sl][:, hh * 128:(hh + 1) * 128],
                        XG[:, hh * 1024 + tc * 512: hh * 1024 + tc * 512 + 512],
                        start=(hh == 0), stop=(hh == 15)))
                    p.emit("te", fn)
                for tc in range(2):
                    fn = (lambda t_, hh=hh, tc=tc, psU=psU, sl=su: nc.tensor.matmul(
                        psU[:, tc * 512:(tc + 1) * 512], USL[sl][:, hh * 128:(hh + 1) * 128],
                        XG[:, hh * 1024 + tc * 512: hh * 1024 + tc * 512 + 512],
                        start=(hh == 0), stop=(hh == 15)))
                    tok = p.inc("te", fn, "TE") if (hh == 15 and tc == 1) else (p.emit("te", fn) or tok)
            gate_free[sg] = tok
            up_free[su] = tok
            if i + 2 < NI:
                fetch_gate(g, i + 2); fetch_up(g, i + 2)
            elif g + 1 < G and i + 2 - NI < 2:
                fetch_gate(g + 1, i + 2 - NI); fetch_up(g + 1, i + 2 - NI)
            p.wait("se", "TE", tok)
            p.wait("se", "VE", prev_sil_tok)
            p.wait("se", "SE", prev_sil_se)
            sil_tok = p.inc("se", lambda s_, psG=psG: s_.activation(
                SIL, psG[:], AF.Silu), "SE")
            prev_sil_se = sil_tok
            p.wait("ve", "SE", sil_tok)
            p.wait("ve", "VE", apr_tok[i])
            a1 = p.inc("ve", lambda v, i=i, psU=psU: v.tensor_tensor(
                out=APR[:, i * 1024:(i + 1) * 1024], in0=psU[:], in1=SIL, op=ALU.mult), "VE")
            prev_sil_tok = a1
            p.wait("ve", "VE", a1)
            p.wait("ve", "VE", state["scbc_ve"])
            apr_tok[i] = p.inc("ve", lambda v, i=i: v.tensor_tensor(
                out=APR[:, i * 1024:(i + 1) * 1024], in0=APR[:, i * 1024:(i + 1) * 1024],
                in1=SCBC, op=ALU.mult), "VE")
        if g + 1 < G:
            emit_scbc_softg(g + 1)
        for hh in range(16):
            kd, vd, sd = down_ld[(g, hh)]
            psD = PS[1] if hh % 2 == 0 else PS[2]
            p.wait("te", kd, vd)
            if hh >= 2:
                p.wait("te", "VE", x2_add[hh - 2])
            elif hh == 0:
                p.wait("te", "VE", apr_tok[NI - 2])
                p.wait("te", "VE", apr_tok[NI - 1])
            elif hh == 1:
                p.wait("te", "VE", apr_tok[NI - 1])
            tok = 0
            for it in range(NI):
                if it == 0 and hh == 0:
                    p.wait("te", "VE", apr_tok[NI - 1])
                for tc in range(2):
                    fn = (lambda t_, it=it, tc=tc, psD=psD, sl=sd: nc.tensor.matmul(
                        psD[:, tc * 512:(tc + 1) * 512], DSL[sl][:, it * 128:(it + 1) * 128],
                        APR[:, it * 1024 + tc * 512: it * 1024 + tc * 512 + 512],
                        start=(it == 0), stop=(it == NI - 1)))
                    tok = p.inc("te", fn, "TE") if (it == NI - 1 and tc == 1) else (p.emit("te", fn) or tok)
            down_free[sd] = tok
            if hh + 2 < 16:
                fetch_down(g, hh + 2)
            elif g + 1 < G:
                fetch_down(g + 1, hh + 2 - 16)
            p.wait("ve", "TE", tok)
            p.wait("ve", "VE", x2_add[hh])
            x2_add[hh] = p.inc("ve", lambda v, hh=hh, psD=psD: v.tensor_tensor(
                out=X1T[:, hh * 1024:(hh + 1) * 1024],
                in0=X1T[:, hh * 1024:(hh + 1) * 1024], in1=psD[:], op=ALU.add), "VE")
            if g + 1 < G:
                if hh < 14:
                    emit_xg_hh(g + 1, hh)
                    if hh == 13:
                        emit_xg_hh(g + 1, 14)
                        emit_xg_hh(g + 1, 15)

    o_r = out.rearrange("(tt p) d -> tt p d", p=128)
    ob_free = [("VE", 0)] * 4
    for tt in range(8):
        par = tt % 4
        ps_pair = (PS[0], PS[3])
        if tt == 0:
            p.wait("te", "VE", apr_tok[NI - 1])
            p.wait("ve", "VE", max(xg_add))
        else:
            p.wait("te", "VE", prev_e2)
        if tt >= 4:
            p.wait("ve", *ob_free[par])
        tok = 0
        for hh in range(16):
            if tt == 0:
                p.wait("te", "VE", x2_add[hh])
            ps = ps_pair[hh // 8]
            fn = (lambda t_, hh=hh, tt=tt, ps=ps: nc.tensor.transpose(
                ps[:, (hh % 8) * 128:(hh % 8 + 1) * 128],
                X1T[:, hh * 1024 + tt * 128: hh * 1024 + tt * 128 + 128], IDNF))
            tok = p.inc("te", fn, "TE") if hh == 15 else (p.emit("te", fn) or tok)
        p.wait("ve", "TE", tok)
        e1 = p.inc("ve", lambda v, par=par: v.tensor_copy(
            out=OUTB[par][:, 0:1024], in_=PS[0][:]), "VE")
        e2 = p.inc("ve", lambda v, par=par: v.tensor_copy(
            out=OUTB[par][:, 1024:2048], in_=PS[3][:]), "VE")
        ob_free[par] = ("VE", e2)
        prev_e2 = e2
        p.wait("sy", "VE", e2)
        od = p.outdma(o_r[tt], OUTB[par], sem=f"OU{par}")
        ob_free[par] = (f"OU{par}", od)
    st["taps_avail"]["XG"] = (XG, BF16, "VE", p.cnt["VE"])
    st["taps_avail"]["APR"] = (APR, BF16, "VE", p.cnt["VE"])
    st["taps_avail"]["SCBC"] = (SCBC, F32, "VE", p.cnt["VE"])
    st["taps_avail"]["SOFTG"] = (SOFTG, BF16, "VE", p.cnt["VE"])
    st["p4_tokens"] = dict(done=True)


def kernel(**inputs):
    shared, per_core = stage_host(inputs)
    nc, _ = build(taps=(), upto=4)
    in_maps = []
    for c in range(8):
        m = dict(shared)
        pc = per_core[c]
        m.update({k: pc[k] for k in ("x_perm", "COS2_K", "SIN2_K", "COS2_Q",
                                     "SIN2_Q", "alibi_k", "alibiq", "clampk",
                                     "masks")})
        in_maps.append({k: np.asarray(v) for k, v in m.items()})
    res = run_bass_kernel_spmd(nc, in_maps, core_ids=list(range(8)))
    outp = np.empty((B, T, D), np.float32)
    for c in range(8):
        o = res.results[c]["out"].reshape(OWN, 128, D)
        for j, pb in enumerate(per_core[c]["own_blocks"]):
            outp[c // 2, pb * 128:(pb + 1) * 128, :] = o[j]
    return outp
